# version 17
# baseline (speedup 1.0000x reference)
import numpy as np
import concourse.bass as bass
import concourse.mybir as mybir

F32 = mybir.dt.float32
BF16 = mybir.dt.bfloat16
AF = mybir.ActivationFunctionType
OP = mybir.AluOpType

EPS = 1e-5


class Geo:
    def __init__(self, C=1024, HID=4096, E=8, F=4096, T2=4096, SEG1=256, SEGA=256,
                 SEGB=128, TSUB=1024, FCH=128):
        self.C, self.HID, self.E, self.F, self.T2 = C, HID, E, F, T2
        self.SEG1, self.SEGA, self.SEGB, self.TSUB, self.FCH = SEG1, SEGA, SEGB, TSUB, FCH
        self.NCI = C // 128
        self.NHI = HID // 128
        self.NFI = F // 128
        self.NS1 = T2 // SEG1
        self.NSA = T2 // SEGA
        self.NSB = T2 // SEGB
        self.NTS = T2 // TSUB


class Prog:
    CENG = ["tensor", "vector", "scalar", "gpsimd", "sync"]

    def __init__(self, nc):
        self.nc = nc
        self.sems, self.counts, self.step = {}, {}, {}
        for e in self.CENG:
            self.sems[e] = nc.alloc_semaphore(f"S_{e}")
            self.counts[e] = 0
            self.step[e] = 1
        self.ops = {e: [] for e in self.CENG}
        self.observed = {e: {} for e in self.CENG}
        self.mem = {}

    def dma_sem(self, name):
        key = "D_" + name
        if key not in self.sems:
            self.sems[key] = self.nc.alloc_semaphore(key)
            self.counts[key] = 0
            self.step[key] = 16
        return key

    def _wait(self, eng, sem_name, cnt):
        if cnt <= 0 or self.observed[eng].get(sem_name, 0) >= cnt:
            return
        self.ops[eng].append(("wait", sem_name, cnt * self.step[sem_name]))
        self.observed[eng][sem_name] = cnt

    def op(self, eng, fn, reads=(), writes=(), queue=None):
        sem_name = queue if queue else eng
        for k in reads:
            st = self.mem.get(k)
            if st:
                for s_, c_ in st["w"].items():
                    self._wait(eng, s_, c_)
        for k in writes:
            st = self.mem.get(k)
            if st:
                for s_, c_ in st["w"].items():
                    self._wait(eng, s_, c_)
                for s_, c_ in st["r"].items():
                    self._wait(eng, s_, c_)
        self.counts[sem_name] += 1
        cnt = self.counts[sem_name]
        self.ops[eng].append(("call", fn, sem_name, cnt))
        for k in writes:
            st = self.mem.setdefault(k, {"w": {}, "r": {}})
            st["w"] = {sem_name: cnt}
            st["r"] = {}
        for k in reads:
            st = self.mem.setdefault(k, {"w": {}, "r": {}})
            st["r"][sem_name] = cnt

    def dma(self, eng, queue, outap, inap, reads=(), writes=()):
        name = writes[0] if writes else "st_" + reads[0]
        self.op(eng, lambda e: e.dma_start(out=outap, in_=inap),
                reads=reads, writes=writes, queue=self.dma_sem(name))

    def barrier(self):
        for e in self.CENG:
            for s_, c_ in self.counts.items():
                self._wait(e, s_, c_)
        self.mem.clear()

    def finish(self):
        for s_, c_ in self.counts.items():
            self._wait("sync", s_, c_)

    def n_items(self):
        return sum(len(v) for v in self.ops.values())

    def emit(self):
        nc = self.nc
        with nc.Block() as block:
            def replay(eng_name):
                def run(eng_obj):
                    for item in self.ops[eng_name]:
                        if item[0] == "wait":
                            _, s_, v = item
                            eng_obj.wait_ge(self.sems[s_], v)
                        else:
                            _, fn, s_, cnt = item
                            ins = fn(eng_obj)
                            ins.then_inc(self.sems[s_], self.step[s_])
                return run
            block.tensor(replay("tensor"))
            block.vector(replay("vector"))
            block.scalar(replay("scalar"))
            block.gpsimd(replay("gpsimd"))
            block.sync(replay("sync"))


class Alloc:
    def __init__(self, nc, budget):
        self.nc = nc
        self.budget = budget

    def region(self, base):
        return Region(self, base)


class Region:
    def __init__(self, alloc, base):
        self.alloc = alloc
        self.cur = base

    def t(self, name, shape, dt):
        nbytes = int(np.prod(shape[1:])) * (2 if dt == BF16 else 4)
        nbytes = (nbytes + 31) & ~31
        h = self.alloc.nc.alloc_sbuf_tensor_at(name, shape, dt, offset=self.cur)
        self.cur += nbytes
        assert self.cur <= self.alloc.budget, f"SBUF overflow at {name}: {self.cur}"
        return h


def build_nc(g: Geo, taps=()):
    nc = bass.Bass("TRN2", target_bir_lowering=False)
    C, NCI, NHI, NFI, T2, E = g.C, g.NCI, g.NHI, g.NFI, g.T2, g.E
    OHW = min(512, C)
    NOH = C // OHW
    taps = set(taps)
    tap_t = {}

    def din(name, shape, dt=F32):
        return nc.dram_tensor(name, shape, dt, kind="ExternalInput").ap()

    x_own = din("x_own", [T2, C])
    x_pref = din("x_pref", [T2, C])
    mask_in = din("mask", [128, 1])
    iota_in = din("iota", [128, max(g.SEG1, 128)])
    identf_in = din("identf", [128, 128])
    onescol_in = din("onescol", [128, 1])
    onesrow_in = din("onesrow", [1, 128])
    fconst_in = din("fconst", [128, 6])
    VEC = ["ln1g", "ln1b", "ln2g", "ln2b", "td", "tf", "tmk", "tmv", "tmr", "cmk", "cmr"]
    vec_in = {n: din("v_" + n, [128, NCI]) for n in VEC}
    wkT_in = din("wkT", [C, C])
    wvT_in = din("wvT", [C, C])
    wrT_in = din("wrT", [C, C])
    woT_in = din("woT", [C, C])
    wckT_in = din("wckT", [C, g.HID])
    wcvT_in = din("wcvT", [g.HID, C])
    wcrT_in = din("wcrT", [C, C])
    wrouter_in = din("wrouter", [C, E])
    wgating_in = din("wgating", [C, E])
    wgate_in = din("wgate", [E, C, g.F])
    wup_in = din("wup", [E, C, g.F])
    wdown_in = din("wdown", [E, g.F, C])

    out_hbm = nc.dram_tensor("out", [T2, C], F32, kind="ExternalOutput").ap()

    xn_hbm = nc.dram_tensor("xn_s", [T2, C], F32, kind="Internal").ap()
    xnT_hbm = nc.dram_tensor("xnT_s", [C, T2], BF16, kind="Internal").ap()
    rwkvT_hbm = nc.dram_tensor("rwkvT_s", [C, T2], BF16, kind="Internal").ap()
    s_hbm = nc.dram_tensor("s_s", [T2, C], F32, kind="Internal").ap()

    def tp(name, shape, dt=F32):
        if name in taps:
            tap_t[name] = nc.dram_tensor("tap_" + name, shape, dt,
                                         kind="ExternalOutput").ap()
            return True
        return False

    SBUF_BASE = 16384
    SBUF_BUDGET = SBUF_BASE + 206 * 1024
    al = Alloc(nc, SBUF_BUDGET)
    R0 = al.region(SBUF_BASE)
    identf = R0.t("identf", [128, 128], F32)
    mask = R0.t("mask", [128, 1], F32)
    iota = R0.t("iota", [128, max(g.SEG1, 128)], F32)
    onescol = R0.t("onescol", [128, 1], F32)
    onesrow = R0.t("onesrow", [1, 128], F32)
    fconst = R0.t("fconst", [128, 6], F32)
    vec = {n: R0.t("v_" + n, [128, NCI], F32) for n in VEC}
    wpos = R0.t("wpos", [128, NCI], F32)
    wneg = R0.t("wneg", [128, NCI], F32)
    dd = R0.t("dd", [128, NCI], F32)
    eu = R0.t("eu", [128, NCI], F32)
    cA = R0.t("cA", [128, NCI], F32)
    cB = R0.t("cB", [128, NCI], F32)
    bdA = R0.t("bdA", [128, NCI], F32)
    bdB = R0.t("bdB", [128, NCI], F32)
    bd_pA = R0.t("bd_pA", [128, NCI], F32)
    bd_pB = R0.t("bd_pB", [128, NCI], F32)
    wbias = R0.t("wbias", [128, NCI], F32)
    h_prevb = R0.t("h_prevb", [128, NCI], BF16)
    h_prev2b = R0.t("h_prev2b", [128, NCI], BF16)
    h2_prevb = R0.t("h2_prevb", [128, NCI], BF16)
    xnbd = R0.t("xnbd", [128, NCI], F32)
    g2m = R0.t("g2m", [128, NCI], F32)
    b2m = R0.t("b2m", [128, NCI], F32)
    bsc = R0.t("bsc", [128, 8 * NCI], F32)
    bscb = R0.t("bscb", [128, 2 * NCI], BF16)
    bsum = R0.t("bsum", [1, 8], F32)
    R0_END = R0.cur
    R1 = al.region(R0_END)
    woT = R1.t("woT", [128, NCI, C], BF16)
    WO_END = R1.cur
    wkT = R1.t("wkT", [128, NCI, C], BF16)
    wvT = R1.t("wvT", [128, NCI, C], BF16)
    wrT = R1.t("wrT", [128, NCI, C], BF16)
    R1_END = R1.cur

    S1 = g.SEG1
    TT1 = S1 // 128
    RA = al.region(R1_END)
    xb = [RA.t(f"xb{i}", [128, TT1, C], F32) for i in range(2)]
    xn_t = RA.t("xn_t", [128, TT1, C], F32)
    stats = RA.t("stats", [128, TT1, 6 * max(1, C // 512)], F32)
    mv = RA.t("mv", [128, TT1, 2], F32)
    rstd = RA.t("rstd", [128, TT1], F32)
    mv2 = RA.t("mv2", [128, TT1, 2], F32)
    rstd2 = RA.t("rstd2", [128, TT1], F32)
    hT = RA.t("hT", [128, NCI, S1 + 1], BF16)
    dmix = RA.t("dmix", [128, NCI, S1], BF16)
    xkT = [RA.t(f"xkT{i}", [128, NCI, S1], BF16) for i in range(2)]
    xvT = [RA.t(f"xvT{i}", [128, NCI, S1], BF16) for i in range(2)]
    xrT = [RA.t(f"xrT{i}", [128, NCI, S1], BF16) for i in range(2)]
    pA = [RA.t(f"pA{i}", [128, NCI, S1], F32) for i in range(2)]
    pB = [RA.t(f"pB{i}", [128, NCI, S1], F32) for i in range(2)]
    srT = [RA.t(f"srT{i}", [128, NCI, S1], BF16) for i in range(2)]
    scanA = RA.t("scanA", [128, NCI, S1 + 1], F32)
    scanB = RA.t("scanB", [128, NCI, S1 + 1], F32)
    gpscr = RA.t("gpscr", [128, S1], BF16)
    numer = RA.t("numer", [128, S1], F32)
    denom = RA.t("denom", [128, S1], F32)
    dinv = RA.t("dinv", [128, S1], F32)
    ybuf = RA.t("ybuf", [128, S1], F32)
    rwkvT = [RA.t(f"rwkvT{i}", [128, NCI, S1], BF16) for i in range(2)]
    xnT_sb = [RA.t(f"xnTsb{i}", [128, NCI, S1], BF16) for i in range(2)]
    cacc = RA.t("cacc", [128, NCI, 2, g.NS1], F32)
    prodscr = [RA.t(f"prodscr{i}", [128, S1], F32) for i in range(2)]
    expw = [RA.t(f"expw{i}", [128, S1], F32) for i in range(2)]
    RA_END = RA.cur

    SB_ = g.SEGB
    R2 = al.region(WO_END)
    wckT = R2.t("wckT", [128, NCI, g.HID], BF16)
    wcvT = R2.t("wcvT", [128, NHI, C], BF16)
    wcrT = R2.t("wcrT", [128, NCI, C], BF16)
    m_xn = [R2.t(f"m_xn{i}", [128, SB_ // 128, C], F32) for i in range(2)]
    m_rw = [R2.t(f"m_rw{i}", [128, NCI, SB_], BF16) for i in range(2)]
    m_x2 = [R2.t(f"m_x2{i}", [128, SB_ // 128, C], F32) for i in range(2)]
    m_stats = R2.t("m_stats", [128, SB_ // 128, 6 * max(1, C // 512)], F32)
    m_mv = R2.t("m_mv", [128, SB_ // 128, 2], F32)
    m_rstd = R2.t("m_rstd", [128, SB_ // 128], F32)
    m_h2T = R2.t("m_h2T", [128, NCI, SB_ + 1], BF16)
    m_dmix = R2.t("m_dmix", [128, NCI, SB_], BF16)
    m_xk2 = [R2.t(f"m_xk2{i}", [128, NCI, SB_], BF16) for i in range(2)]
    m_xr2 = [R2.t(f"m_xr2{i}", [128, NCI, SB_], BF16) for i in range(2)]
    m_kkh = [R2.t(f"m_kkh{i}", [128, NHI // 4, SB_], BF16) for i in range(2)]
    m_rel = [R2.t(f"m_rel{i}", [128, 2 * SB_], BF16) for i in range(2)]
    m_srk = R2.t("m_srk", [128, SB_ // 128, C], BF16)
    m_rkv = R2.t("m_rkv", [128, SB_ // 128, C], BF16)
    assert R2.cur <= al.budget, f"A2 region overflow {R2.cur}"

    TS = g.TSUB
    NTT = TS // 128
    FCH = g.FCH
    R3 = al.region(R0_END)
    f_xnT = [R3.t("f_xnT0", [128, NCI, TS], BF16)] * 2
    f_hact = R3.t("f_hact", [128, NFI, TS], BF16)
    f_acc = R3.t("f_acc", [128, NTT, C], F32)
    f_wg = [R3.t(f"f_wg{i}", [128, NCI, FCH], BF16) for i in range(2)]
    f_wu = [R3.t(f"f_wu{i}", [128, NCI, FCH], BF16) for i in range(2)]
    f_wd = [R3.t(f"f_wd{i}", [128, NFI // 4, OHW], BF16) for i in range(6)]
    f_sil = [R3.t(f"f_sil{i}", [128, 512], BF16) for i in range(2)]
    f_cw = R3.t("f_cw", [128, (T2 // 128), E], F32)
    f_smx = R3.t("f_smx", [128, 4], F32)
    f_wr = R3.t("f_wr", [128, NCI, E], BF16)
    f_wgt = R3.t("f_wgt", [128, NCI, E], BF16)
    f_x = [R3.t(f"f_x{i}", [128, C], F32) for i in range(2)]
    f_s = [R3.t(f"f_s{i}", [128, C], F32) for i in range(2)]
    f_g = [R3.t(f"f_g{i}", [128, C], F32) for i in range(2)]
    f_o = [R3.t("f_o0", [128, C], F32)] * 2
    assert R3.cur <= al.budget, f"F region overflow {R3.cur}"

    pbank = [nc.alloc_psum_tensor(f"pb{i}", [128, 512], F32) for i in range(8)]

    P = Prog(nc)

    for t, src, key in ([(identf, identf_in, "identf"), (mask, mask_in, "mask"),
                         (iota, iota_in, "iota"), (onescol, onescol_in, "onescol"),
                         (onesrow, onesrow_in, "onesrow"), (fconst, fconst_in, "fconst")] +
                        [(vec[n], vec_in[n], "v_" + n) for n in VEC]):
        P.dma("sync", "qsync", t[:], src, writes=(key,))
    for t, src, key in [(wkT, wkT_in, "wkT"), (wvT, wvT_in, "wvT"),
                        (wrT, wrT_in, "wrT"), (woT, woT_in, "woT")]:
        P.dma("gpsimd", "qgps", t[:], src.rearrange("(a p) o -> p a o", p=128),
              writes=(key,))
    P.barrier()

    P.op("scalar", lambda s: s.activation(out=wpos[:], in_=vec["td"][:], func=AF.Exp, bias=fconst[:, 4:5], scale=fconst[:, 5:6]),
         reads=("v_td",), writes=("wpos",))
    P.op("vector", lambda v: v.tensor_scalar_mul(wneg[:], wpos[:], -1.0),
         reads=("wpos",), writes=("wneg",))
    P.op("scalar", lambda s: s.activation(out=dd[:], in_=wpos[:], func=AF.Exp,
                                          scale=fconst[:, 2:3], bias=fconst[:, 4:5]),
         reads=("wpos", "fconst"), writes=("dd",))
    P.op("scalar", lambda s: s.activation(out=eu[:], in_=vec["tf"][:], func=AF.Exp, bias=fconst[:, 4:5], scale=fconst[:, 5:6]),
         reads=("v_tf",), writes=("eu",))
    P.op("vector", lambda v: v.memset(hT[:, :, 0:1], 0.0), writes=("hT_c0",))
    P.op("vector", lambda v: v.tensor_scalar_mul(g2m[:], vec["ln2g"][:], mask[:, 0:1]),
         reads=("v_ln2g", "mask"), writes=("g2m",))
    P.op("vector", lambda v: v.tensor_scalar_mul(b2m[:], vec["ln2b"][:], mask[:, 0:1]),
         reads=("v_ln2b", "mask"), writes=("b2m",))

    def ln_stats(xkey, xview, statt, statkey, mvten, mvkey, rstdten, rstdkey, tt):
        nch = max(1, C // 512)
        chw = min(512, C)
        for j in range(nch):
            P.op("vector", lambda v, j=j: v.bn_stats(
                statt[:, tt, 6 * j:6 * (j + 1)], xview[:, j * chw:(j + 1) * chw]),
                reads=(xkey,), writes=(statkey,))
        P.op("vector", lambda v: v.bn_aggr(mvten[:, tt, :], statt[:, tt, :]),
             reads=(statkey,), writes=(mvkey,))
        P.op("scalar", lambda s: s.activation(out=rstdten[:, tt:tt + 1],
                                              in_=mvten[:, tt, 1:2], func=AF.Ln,
                                              bias=fconst[:, 0:1], scale=fconst[:, 5:6]),
             reads=(mvkey, "fconst"), writes=(rstdkey + "_t",))
        P.op("scalar", lambda s: s.activation(out=rstdten[:, tt:tt + 1],
                                              in_=rstdten[:, tt:tt + 1], func=AF.Exp,
                                              scale=fconst[:, 1:2], bias=fconst[:, 4:5]),
             reads=(rstdkey + "_t", "fconst"), writes=(rstdkey,))

    def transpose_to(src, srckey, dst, dstkey, ttn, col_off, banks, gb=None, extra_reads=()):
        for ci in range(NCI):
            bi = banks[ci % len(banks)]
            bank, bkey = pbank[bi], f"pb{bi}"
            def tfn(t, ci=ci, bank=bank):
                ins = None
                for tt in range(ttn):
                    ins = t.transpose(bank[:, tt * 128:(tt + 1) * 128],
                                      src[:, tt, ci * 128:(ci + 1) * 128], identf[:])
                return ins
            P.op("tensor", tfn, reads=(srckey, "identf") + tuple(extra_reads), writes=(bkey,))
            if gb is None:
                P.op("scalar", lambda s_, ci=ci, bank=bank: s_.copy(
                    out=dst[:, ci, col_off:col_off + ttn * 128], in_=bank[:, 0:ttn * 128]),
                    reads=(bkey,), writes=(dstkey,))
            else:
                gv, bv, gk, bk2 = gb
                P.op("scalar", lambda s_, ci=ci, bank=bank, gv=gv, bv=bv: s_.activation(
                    out=dst[:, ci, col_off:col_off + ttn * 128], in_=bank[:, 0:ttn * 128],
                    func=AF.Identity, bias=bv[:, ci:ci + 1], scale=gv[:, ci:ci + 1]),
                    reads=(bkey, gk, bk2), writes=(dstkey,))

    def lp(fn):
        def wrapped(e):
            with nc.allow_low_precision(reason="bf16 store on matmul-operand path"):
                return fn(e)
        return wrapped

    def mm_chain(psum_ap, psum_key, pairs, rkeys, start=True, stop=True):
        n = len(pairs)
        def fn(t):
            ins = None
            for i, (l, r) in enumerate(pairs):
                ins = t.matmul(psum_ap, l, r, start=(start and i == 0),
                               stop=(stop and i == n - 1))
            return ins
        P.op("tensor", fn, reads=tuple(rkeys), writes=(psum_key,))

    def a1_stageA(s, is_prefix):
        par = s % 2
        xkey = f"xb{par}"
        src = x_pref if is_prefix else x_own
        seg0 = s * S1
        P.dma("sync", "qsync", xb[par][:],
              src[seg0:seg0 + S1, :].rearrange("(a p) c -> p a c", p=128),
              writes=(xkey,))

        nchv = max(1, C // 512)
        chw = min(512, C)
        for tt in range(TT1):
            for j in range(nchv):
                P.op("vector", lambda v, tt=tt, j=j, par=par: v.bn_stats(
                    stats[:, tt, 6 * j:6 * (j + 1)], xb[par][:, tt, j * chw:(j + 1) * chw]),
                    reads=(xkey,), writes=("stats",))
            P.op("vector", lambda v, tt=tt: v.bn_aggr(mv[:, tt, :], stats[:, tt, :]),
                 reads=("stats",), writes=("mv",))
        P.op("scalar", lambda s: s.activation(out=rstd[:, :], in_=mv[:, :, 1],
                                              func=AF.Ln, bias=fconst[:, 0:1],
                                              scale=fconst[:, 5:6]),
             reads=("mv", "fconst"), writes=("rstd_t",))
        P.op("scalar", lambda s: s.activation(out=rstd[:, :], in_=rstd[:, :],
                                              func=AF.Exp, scale=fconst[:, 1:2],
                                              bias=fconst[:, 4:5]),
             reads=("rstd_t", "fconst"), writes=("rstd",))
        for tt in range(TT1):
            P.op("vector", lambda v, tt=tt, par=par: v.tensor_scalar(
                xn_t[:, tt, :], xb[par][:, tt, :], mv[:, tt, 0:1], rstd[:, tt:tt + 1],
                OP.subtract, OP.mult),
                reads=(xkey, "mv", "rstd"), writes=("xn_t",))
            for j in range(nchv):
                P.op("vector", lambda v, tt=tt, j=j: v.bn_stats(
                    stats[:, tt, 6 * j:6 * (j + 1)], xn_t[:, tt, j * chw:(j + 1) * chw]),
                    reads=("xn_t",), writes=("stats2",))
            P.op("vector", lambda v, tt=tt: v.bn_aggr(mv2[:, tt, :], stats[:, tt, :]),
                 reads=("stats2",), writes=("mv2",))
        P.op("scalar", lambda s: s.activation(out=rstd2[:, :], in_=mv2[:, :, 1],
                                              func=AF.Ln, bias=fconst[:, 0:1],
                                              scale=fconst[:, 5:6]),
             reads=("mv2", "fconst"), writes=("rstd2_t",))
        P.op("scalar", lambda s: s.activation(out=rstd2[:, :], in_=rstd2[:, :],
                                              func=AF.Exp, scale=fconst[:, 1:2],
                                              bias=fconst[:, 4:5]),
             reads=("rstd2_t", "fconst"), writes=("rstd2",))
        for tt in range(TT1):
            P.op("vector", lambda v, tt=tt, par=par: v.tensor_scalar(
                xb[par][:, tt, :], xn_t[:, tt, :], mv2[:, tt, 0:1], rstd2[:, tt:tt + 1],
                OP.subtract, OP.mult),
                reads=("xn_t", "mv2", "rstd2"), writes=(xkey,))

        if not is_prefix:
            P.dma("sync", "qsync",
                  xn_hbm[seg0:seg0 + S1, :].rearrange("(a p) c -> p a c", p=128),
                  xn_t[:], reads=("xn_t",))
            transpose_to(xn_t, "xn_t", xnT_sb[par], f"xnT{par}", TT1, 0, (0, 1))
            P.dma("sync", "qsync",
                  xnT_hbm.rearrange("(a p) t -> p a t", p=128)[:, :, seg0:seg0 + S1],
                  xnT_sb[par][:], reads=(f"xnT{par}",))

        transpose_to(xb[par], xkey, hT, "hT", TT1, 1, (2, 3),
                     gb=(vec["ln1g"], vec["ln1b"], "v_ln1g", "v_ln1b"))

        P.op("gpsimd", lambda gp: gp.tensor_sub(
            dmix[:, :, :], hT[:, :, 1:S1 + 1], hT[:, :, 0:S1]),
            reads=("hT", "hT_c0"), writes=("dmix",))
        for ci in range(NCI):
            for dst, dk, tmn, eng in ([(xkT[par], f"xkT{par}", "tmk", "vector"),
                                       (xvT[par], f"xvT{par}", "tmv", "gpsimd")] +
                                      ([(xrT[par], f"xrT{par}", "tmr", "vector")]
                                       if not is_prefix else [])):
                if eng == "gpsimd":
                    P.op("gpsimd", lambda gp, ci=ci, tmn=tmn: gp.tensor_mul(
                        gpscr[:, :], dmix[:, ci, :],
                        vec[tmn][:, ci:ci + 1].broadcast_to([128, S1])),
                        reads=("dmix", "v_" + tmn), writes=("gpscr",))
                    P.op("gpsimd", lambda gp, ci=ci, dst=dst: gp.tensor_add(
                        dst[:, ci, :], gpscr[:, :], hT[:, ci, 0:S1]),
                        reads=("gpscr", "hT", "hT_c0"), writes=(dk,))
                else:
                    P.op(eng, lambda v, ci=ci, dst=dst, tmn=tmn: v.scalar_tensor_tensor(
                        dst[:, ci, :], dmix[:, ci, :], vec[tmn][:, ci:ci + 1],
                        hT[:, ci, 0:S1], OP.mult, OP.add),
                        reads=("dmix", "v_" + tmn, "hT", "hT_c0"), writes=(dk,))

        if is_prefix and s == g.NS1 - 1:
            P.op("vector", lambda v: v.tensor_copy(h_prevb[:, :],
                                                   hT[:, :, S1:S1 + 1].squeeze(2)),
                 reads=("hT",), writes=("h_prevb",))
            P.op("vector", lambda v: v.tensor_copy(h_prev2b[:, :],
                                                   hT[:, :, S1 - 1:S1].squeeze(2)),
                 reads=("hT",), writes=("h_prev2b",))
        P.op("vector", lambda v: v.tensor_copy(hT[:, :, 0:1], hT[:, :, S1:S1 + 1]),
             reads=("hT",), writes=("hT_c0",))


    def a1_stageB(s, is_prefix):
        par = s % 2
        seg0 = s * S1
        for oi in range(NCI):
            bki = 4 + (oi % 4)
            bk, bkkey = pbank[bki], f"pb{bki}"
            mm_chain(bk[:, 0:S1], bkkey,
                     [(wkT[:, ci, oi * 128:(oi + 1) * 128], xkT[par][:, ci, :])
                      for ci in range(NCI)], ("wkT", f"xkT{par}"))
            P.op("scalar", lambda s_, oi=oi, bk=bk, par=par: s_.activation(
                out=pB[par][:, oi, :], in_=bk[:, 0:S1], func=AF.Exp, bias=fconst[:, 4:5], scale=fconst[:, 5:6]),
                reads=(bkkey,), writes=(f"pB{par}",))
            bvi = (oi % 4)
            bv, bvkey = pbank[bvi], f"pb{bvi}"
            mm_chain(bv[:, 0:S1], bvkey,
                     [(wvT[:, ci, oi * 128:(oi + 1) * 128], xvT[par][:, ci, :])
                      for ci in range(NCI)], ("wvT", f"xvT{par}"))
            P.op("vector", lambda v, oi=oi, bv=bv, par=par: v.tensor_mul(
                pA[par][:, oi, :], bv[:, 0:S1], pB[par][:, oi, :]),
                reads=(bvkey, f"pB{par}"), writes=(f"pA{par}",))
        for oi in range(NCI):
            if not is_prefix:
                bki = 4 + (oi % 4)
                bk, bkkey = pbank[bki], f"pb{bki}"
                mm_chain(bk[:, 0:S1], bkkey,
                         [(wrT[:, ci, oi * 128:(oi + 1) * 128], xrT[par][:, ci, :])
                          for ci in range(NCI)], ("wrT", f"xrT{par}"))
                P.op("scalar", lambda s_, oi=oi, bk=bk, par=par: s_.activation(
                    out=srT[par][:, oi, :], in_=bk[:, 0:S1], func=AF.Exp, scale=fconst[:, 2:3], bias=fconst[:, 4:5]),
                    reads=(bkkey, "fconst"), writes=(f"srT{par}",))

        if not is_prefix:
            P.op("vector", lp(lambda v, par=par: v.tensor_scalar_add(
                srT[par][:, :, :], srT[par][:, :, :], 1.0)),
                reads=(f"srT{par}",), writes=(f"srT{par}",))
            P.op("vector", lp(lambda v, par=par: v.reciprocal(
                srT[par].ap().rearrange("p a b -> p (a b)"),
                srT[par].ap().rearrange("p a b -> p (a b)"))),
                reads=(f"srT{par}",), writes=(f"srT{par}",))

        if is_prefix:
            FD = S1 - 1 if s == g.NS1 - 1 else S1
            P.op("vector", lambda v, s=s: v.tensor_scalar_mul(
                wbias[:, :], wneg[:, :], float(T2 - 2 - s * S1)),
                reads=("wneg",), writes=("wbias",))
            for oi in range(NCI):
                ep = oi % 2
                P.op("scalar", lambda s_, oi=oi, FD=FD, ep=ep: s_.activation(
                    out=expw[ep][:, 0:FD], in_=iota[:, 0:FD], func=AF.Exp,
                    bias=wbias[:, oi:oi + 1], scale=wpos[:, oi:oi + 1]),
                    reads=("iota", "wbias", "wpos"), writes=(f"expw{ep}",))
                for pten, pk, col in [(pA[par], f"pA{par}", 0), (pB[par], f"pB{par}", 1)]:
                    P.op("vector", lambda v, oi=oi, pten=pten, col=col, FD=FD, ep=ep, s=s:
                         v.scalar_tensor_tensor(
                             prodscr[col][:, 0:FD], expw[ep][:, 0:FD], 1.0,
                             pten[:, oi, 0:FD], OP.mult, OP.mult,
                             accum_out=cacc[:, oi, col, s:s + 1]),
                         reads=(f"expw{ep}", pk), writes=(f"prodscr{col}", "cacc"))
            if s == g.NS1 - 1:
                P.op("vector", lambda v, par=par: v.tensor_copy(
                    bd_pA[:, :], pA[par][:, :, S1 - 1:S1].squeeze(2)),
                     reads=(f"pA{par}",), writes=("bd_pA",))
                P.op("vector", lambda v, par=par: v.tensor_copy(
                    bd_pB[:, :], pB[par][:, :, S1 - 1:S1].squeeze(2)),
                     reads=(f"pB{par}",), writes=("bd_pB",))
                for ci in range(NCI):
                    P.op("tensor", lambda t, ci=ci: t.transpose(
                        pbank[0][:, 0:128], xn_t[:, TT1 - 1, ci * 128:(ci + 1) * 128],
                        identf[:]),
                        reads=("xn_t", "identf"), writes=("pb0",))
                    P.op("vector", lambda v, ci=ci: v.tensor_copy(
                        xnbd[:, ci:ci + 1], pbank[0][:, 127:128]),
                        reads=("pb0",), writes=("xnbd",))
        else:
            P.op("vector", lambda v: v.tensor_copy(
                scanA[:, :, 0:1], cA[:, :].unsqueeze(2)), reads=("cA",), writes=("scanA",))
            P.op("vector", lambda v: v.tensor_copy(
                scanB[:, :, 0:1], cB[:, :].unsqueeze(2)), reads=("cB",), writes=("scanB",))
            for oi in range(NCI):
                P.op("vector", lambda v, oi=oi, par=par: v.tensor_tensor_scan(
                    scanA[:, oi, 1:S1 + 1], dd[:, oi:oi + 1].broadcast_to([128, S1]),
                    pA[par][:, oi, :], cA[:, oi:oi + 1], OP.mult, OP.add),
                    reads=("dd", f"pA{par}", "cA"), writes=("scanA",))
                P.op("vector", lambda v, oi=oi, par=par: v.tensor_tensor_scan(
                    scanB[:, oi, 1:S1 + 1], dd[:, oi:oi + 1].broadcast_to([128, S1]),
                    pB[par][:, oi, :], cB[:, oi:oi + 1], OP.mult, OP.add),
                    reads=("dd", f"pB{par}", "cB"), writes=("scanB",))
                P.op("vector", lambda v, oi=oi, par=par: v.scalar_tensor_tensor(
                    scanA[:, oi, 0:S1], pA[par][:, oi, :], eu[:, oi:oi + 1],
                    scanA[:, oi, 0:S1],
                    OP.mult, OP.add), reads=(f"pA{par}", "eu", "scanA"), writes=("scanA",))
                P.op("vector", lambda v, oi=oi, par=par: v.scalar_tensor_tensor(
                    scanB[:, oi, 0:S1], pB[par][:, oi, :], eu[:, oi:oi + 1],
                    scanB[:, oi, 0:S1],
                    OP.mult, OP.add), reads=(f"pB{par}", "eu", "scanB"), writes=("scanB",))
            P.op("vector", lambda v: v.reciprocal(
                scanB[:, :, 0:S1], scanB[:, :, 0:S1]),
                reads=("scanB",), writes=("scanB",))
            P.op("gpsimd", lambda gp: gp.tensor_mul(
                scanA[:, :, 0:S1], scanA[:, :, 0:S1], scanB[:, :, 0:S1]),
                reads=("scanA", "scanB"), writes=("scanA",))
            P.op("gpsimd", lambda gp, par=par: gp.tensor_mul(
                rwkvT[par][:, :, :], scanA[:, :, 0:S1], srT[par][:, :, :]),
                reads=("scanA", f"srT{par}"), writes=(f"rwkvT{par}",))
            P.op("vector", lambda v: v.tensor_copy(
                cA[:, :], scanA[:, :, S1:S1 + 1].squeeze(2)), reads=("scanA",), writes=("cA",))
            P.op("vector", lambda v: v.tensor_copy(
                cB[:, :], scanB[:, :, S1:S1 + 1].squeeze(2)), reads=("scanB",), writes=("cB",))
            P.dma("sync", "qsync",
                  rwkvT_hbm.rearrange("(a p) t -> p a t", p=128)[:, :, seg0:seg0 + S1],
                  rwkvT[par][:], reads=(f"rwkvT{par}",))

    def a1_boundary():
        t0 = bsc[:, 0:NCI]
        P.op("vector", lambda v: v.tensor_sub(t0, h_prevb[:], h_prev2b[:]),
             reads=("h_prevb", "h_prev2b"), writes=("bt0",))
        P.op("vector", lambda v: v.tensor_mul(t0, t0, vec["tmr"][:]),
             reads=("bt0", "v_tmr"), writes=("bt0",))
        xrbd = bscb[:, 0:NCI]
        P.op("vector", lambda v: v.tensor_add(xrbd, t0, h_prev2b[:]),
             reads=("bt0", "h_prev2b"), writes=("xrbd",))
        for oi in range(NCI):
            mm_chain(pbank[0][:, oi:oi + 1], "pb0",
                     [(wrT[:, ci, oi * 128:(oi + 1) * 128], xrbd[:, ci:ci + 1])
                      for ci in range(NCI)], ("wrT", "xrbd"))
        srbd = bsc[:, NCI:2 * NCI]
        P.op("scalar", lambda s: s.activation(out=srbd, in_=pbank[0][:, 0:NCI],
                                              func=AF.Exp, scale=fconst[:, 2:3], bias=fconst[:, 4:5]),
             reads=("pb0", "fconst"), writes=("srbd",))
        P.op("vector", lambda v: v.tensor_scalar_add(srbd, srbd, 1.0),
             reads=("srbd",), writes=("srbd",))
        P.op("vector", lambda v: v.reciprocal(srbd, srbd), reads=("srbd",), writes=("srbd",))
        ybd = bsc[:, 2 * NCI:3 * NCI]
        dbd = bsc[:, 3 * NCI:4 * NCI]
        for oi in range(NCI):
            P.op("vector", lambda v, oi=oi: v.scalar_tensor_tensor(
                ybd[:, oi:oi + 1], bd_pA[:, oi:oi + 1], eu[:, oi:oi + 1], bdA[:, oi:oi + 1],
                OP.mult, OP.add), reads=("bd_pA", "eu", "bdA"), writes=("ybd",))
            P.op("vector", lambda v, oi=oi: v.scalar_tensor_tensor(
                dbd[:, oi:oi + 1], bd_pB[:, oi:oi + 1], eu[:, oi:oi + 1], bdB[:, oi:oi + 1],
                OP.mult, OP.add), reads=("bd_pB", "eu", "bdB"), writes=("dbd",))
        P.op("vector", lambda v: v.reciprocal(dbd, dbd), reads=("dbd",), writes=("dbd",))
        P.op("vector", lambda v: v.tensor_mul(ybd, ybd, dbd), reads=("ybd", "dbd"),
             writes=("ybd",))
        rwbd = bscb[:, NCI:2 * NCI]
        P.op("vector", lambda v: v.tensor_mul(rwbd, ybd, srbd),
             reads=("ybd", "srbd"), writes=("rwbd",))
        for oi in range(NCI):
            mm_chain(pbank[1][:, oi:oi + 1], "pb1",
                     [(woT[:, ci, oi * 128:(oi + 1) * 128], rwbd[:, ci:ci + 1])
                      for ci in range(NCI)], ("woT", "rwbd"))
        x2bd = bsc[:, 4 * NCI:5 * NCI]
        P.op("vector", lambda v: v.tensor_add(x2bd, xnbd[:], pbank[1][:, 0:NCI]),
             reads=("xnbd", "pb1"), writes=("x2bd",))
        mm_chain(pbank[2][0:1, 0:1], "pb2",
                 [(x2bd[:, ci:ci + 1], onescol[:]) for ci in range(NCI)],
                 ("x2bd", "onescol"))
        mm_chain(pbank[2][0:1, 1:2], "pb2b",
                 [(x2bd[:, ci:ci + 1], x2bd[:, ci:ci + 1]) for ci in range(NCI)],
                 ("x2bd",))
        P.op("vector", lambda v: v.tensor_scalar_mul(bsum[0:1, 0:2], pbank[2][0:1, 0:2],
                                                     1.0 / C),
             reads=("pb2", "pb2b"), writes=("bsum",))
        P.op("vector", lambda v: v.tensor_mul(bsum[0:1, 2:3], bsum[0:1, 0:1],
                                              bsum[0:1, 0:1]),
             reads=("bsum",), writes=("bsum2",))
        P.op("vector", lambda v: v.tensor_sub(bsum[0:1, 3:4], bsum[0:1, 1:2],
                                              bsum[0:1, 2:3]),
             reads=("bsum", "bsum2"), writes=("bvar",))
        P.op("scalar", lambda s: s.activation(out=bsum[0:1, 4:5], in_=bsum[0:1, 3:4],
                                              func=AF.Ln, bias=fconst[0:1, 0:1], scale=fconst[0:1, 5:6]),
             reads=("bvar", "fconst"), writes=("blnv",))
        P.op("scalar", lambda s: s.activation(out=bsum[0:1, 1:2], in_=bsum[0:1, 4:5],
                                              func=AF.Exp, scale=fconst[0:1, 1:2], bias=fconst[0:1, 4:5]),
             reads=("blnv", "fconst"), writes=("brstd",))
        mm_chain(pbank[3][:, 0:2], "pb3", [(onesrow[:], bsum[0:1, 0:2])],
                 ("onesrow", "bsum", "brstd"))
        for ci in range(NCI):
            P.op("vector", lambda v, ci=ci: v.tensor_scalar(
                t0[:, ci:ci + 1], x2bd[:, ci:ci + 1], pbank[3][:, 0:1], pbank[3][:, 1:2],
                OP.subtract, OP.mult),
                reads=("x2bd", "pb3"), writes=("bt0",))
        P.op("vector", lambda v: v.scalar_tensor_tensor(
            h2_prevb[:, :], t0, 1.0, g2m[:], OP.mult, OP.mult),
            reads=("bt0", "g2m"), writes=("h2_prevb_t",))
        P.op("vector", lambda v: v.tensor_add(h2_prevb[:, :], h2_prevb[:, :], b2m[:]),
             reads=("h2_prevb_t", "b2m"), writes=("h2_prevb",))
        for cdst, ckey, bsrc, bk, psrc, pk in [(cA, "cA", bdA, "bdA", bd_pA, "bd_pA"),
                                               (cB, "cB", bdB, "bdB", bd_pB, "bd_pB")]:
            P.op("vector", lambda v, cdst=cdst, bsrc=bsrc, psrc=psrc: v.tensor_mul(
                cdst[:], dd[:], bsrc[:]), reads=("dd", bk), writes=(ckey,))
            P.op("vector", lambda v, cdst=cdst, psrc=psrc: v.tensor_add(
                cdst[:], cdst[:], psrc[:]), reads=(ckey, pk), writes=(ckey,))
            P.op("vector", lambda v, cdst=cdst: v.tensor_scalar_mul(
                cdst[:], cdst[:], mask[:, 0:1]), reads=(ckey, "mask"), writes=(ckey,))
        P.op("vector", lambda v: v.tensor_scalar_mul(
            hT[:, :, 0:1], h_prevb[:, :].unsqueeze(2), mask[:, 0:1]),
            reads=("h_prevb", "mask"), writes=("hT_c0",))

    a1_stageA(0, True)
    for s in range(1, g.NS1):
        a1_stageA(s, True)
        a1_stageB(s - 1, True)
    a1_stageB(g.NS1 - 1, True)
    for oi in range(NCI):
        P.op("vector", lambda v, oi=oi: v.reduce_sum(
            bdA[:, oi:oi + 1], cacc[:, oi, 0, :], axis=mybir.AxisListType.X),
            reads=("cacc",), writes=("bdA",))
        P.op("vector", lambda v, oi=oi: v.reduce_sum(
            bdB[:, oi:oi + 1], cacc[:, oi, 1, :], axis=mybir.AxisListType.X),
            reads=("cacc",), writes=("bdB",))
    a1_boundary()
    a1_stageA(0, False)
    for s in range(1, g.NS1):
        a1_stageA(s, False)
        a1_stageB(s - 1, False)
    a1_stageB(g.NS1 - 1, False)
    if tp("bdA", [128, NCI]):
        P.dma("sync", "qsync", tap_t["bdA"], bdA[:], reads=("bdA",))
    if tp("cA", [128, NCI]):
        P.dma("sync", "qsync", tap_t["cA"], cA[:], reads=("cA",))
    if tp("h2_prevb", [128, NCI]):
        P.op("vector", lambda v: v.tensor_copy(bsc[:, 0:NCI], h2_prevb[:]),
             reads=("h2_prevb",), writes=("bt0",))
        P.dma("sync", "qsync", tap_t["h2_prevb"], bsc[:, 0:NCI], reads=("bt0",))

    P.barrier()

    for t, srcw, key in [(wckT, wckT_in, "wckT"), (wcvT, wcvT_in, "wcvT"),
                         (wcrT, wcrT_in, "wcrT")]:
        P.dma("gpsimd", "qgps", t[:], srcw.rearrange("(a p) o -> p a o", p=128),
              writes=(key,))

    def a2_stage1(s):
        par = s % 2
        seg0 = s * SB_
        P.dma("sync", "qsync", m_xn[par][:],
              xn_hbm[seg0:seg0 + SB_, :].rearrange("(a p) c -> p a c", p=128),
              writes=(f"m_xn{par}",))
        P.dma("sync", "qsync", m_rw[par][:],
              rwkvT_hbm.rearrange("(a p) t -> p a t", p=128)[:, :, seg0:seg0 + SB_],
              writes=(f"m_rw{par}",))
        for oh in range(NOH):
            bi = (0, 1)[oh % 2]
            bank, bkey = pbank[bi], f"pb{bi}"
            mm_chain(bank[:, 0:OHW], bkey,
                     [(m_rw[par][:, ci, 0:128], woT[:, ci, oh * OHW:(oh + 1) * OHW])
                      for ci in range(NCI)], (f"m_rw{par}", "woT"))
            P.op("vector", lambda v, oh=oh, bank=bank, par=par: v.tensor_add(
                m_x2[par][:, 0, oh * OHW:(oh + 1) * OHW],
                m_xn[par][:, 0, oh * OHW:(oh + 1) * OHW], bank[:, 0:OHW]),
                reads=(f"m_xn{par}", bkey), writes=(f"m_x2{par}",))

    def a2_stage2(s):
        par = s % 2
        seg0 = s * SB_
        nchv = max(1, C // 512)
        chw = min(512, C)
        for j in range(nchv):
            P.op("vector", lambda v, j=j, par=par: v.bn_stats(
                m_stats[:, 0, 6 * j:6 * (j + 1)], m_x2[par][:, 0, j * chw:(j + 1) * chw]),
                reads=(f"m_x2{par}",), writes=("m_stats",))
        P.op("vector", lambda v: v.bn_aggr(m_mv[:, 0, :], m_stats[:, 0, :]),
             reads=("m_stats",), writes=("m_mv",))
        P.op("scalar", lambda s_: s_.activation(out=m_rstd[:, :], in_=m_mv[:, :, 1],
                                                func=AF.Ln, bias=fconst[:, 0:1],
                                                scale=fconst[:, 5:6]),
             reads=("m_mv", "fconst"), writes=("m_rstd_t",))
        P.op("scalar", lambda s_: s_.activation(out=m_rstd[:, :], in_=m_rstd[:, :],
                                                func=AF.Exp, scale=fconst[:, 1:2],
                                                bias=fconst[:, 4:5]),
             reads=("m_rstd_t", "fconst"), writes=("m_rstd",))
        P.op("vector", lambda v, par=par: v.tensor_scalar(
            m_xn[par][:, 0, :], m_x2[par][:, 0, :], m_mv[:, 0, 0:1], m_rstd[:, 0:1],
            OP.subtract, OP.mult),
            reads=(f"m_x2{par}", "m_mv", "m_rstd"), writes=(f"m_xn{par}",))
        if s == 0:
            P.op("vector", lambda v: v.tensor_copy(m_h2T[:, :, 0:1],
                                                   h2_prevb[:, :].unsqueeze(2)),
                 reads=("h2_prevb",), writes=("m_h2T_c0",))
        for ci in range(NCI):
            bi = (2, 3)[ci % 2]
            bank, bkey = pbank[bi], f"pb{bi}"
            P.op("tensor", lambda t, ci=ci, bank=bank, par=par: t.transpose(
                bank[:, 0:128], m_xn[par][:, 0, ci * 128:(ci + 1) * 128], identf[:]),
                reads=(f"m_xn{par}", "identf"), writes=(bkey,))
            P.op("scalar", lambda s_, ci=ci, bank=bank: s_.activation(
                out=m_h2T[:, ci, 1:SB_ + 1], in_=bank[:, 0:128],
                func=AF.Identity, bias=vec["ln2b"][:, ci:ci + 1],
                scale=vec["ln2g"][:, ci:ci + 1]),
                reads=(bkey, "v_ln2g", "v_ln2b"), writes=("m_h2T",))
        P.op("gpsimd", lambda gp: gp.tensor_sub(
            m_dmix[:, :, :], m_h2T[:, :, 1:SB_ + 1], m_h2T[:, :, 0:SB_]),
            reads=("m_h2T", "m_h2T_c0"), writes=("m_dmix",))
        for ci in range(NCI):
            P.op("vector", lambda v, ci=ci, par=par: v.scalar_tensor_tensor(
                m_xk2[par][:, ci, :], m_dmix[:, ci, :], vec["cmk"][:, ci:ci + 1],
                m_h2T[:, ci, 0:SB_], OP.mult, OP.add),
                reads=("m_dmix", "v_cmk", "m_h2T", "m_h2T_c0"), writes=(f"m_xk2{par}",))
            P.op("vector", lambda v, ci=ci, par=par: v.scalar_tensor_tensor(
                m_xr2[par][:, ci, :], m_dmix[:, ci, :], vec["cmr"][:, ci:ci + 1],
                m_h2T[:, ci, 0:SB_], OP.mult, OP.add),
                reads=("m_dmix", "v_cmr", "m_h2T", "m_h2T_c0"), writes=(f"m_xr2{par}",))
        P.op("vector", lambda v: v.tensor_copy(m_h2T[:, :, 0:1], m_h2T[:, :, SB_:SB_ + 1]),
             reads=("m_h2T",), writes=("m_h2T_c0",))

    def a2_stage3(s):
        par = s % 2
        seg0 = s * SB_
        for oh in range(NOH):
            bank, bkey = pbank[3], "pb3"
            mm_chain(bank[:, 0:OHW], bkey,
                     [(m_xr2[par][:, ci, 0:128], wcrT[:, ci, oh * OHW:(oh + 1) * OHW])
                      for ci in range(NCI)], (f"m_xr2{par}", "wcrT"))
            P.op("scalar", lp(lambda s_, oh=oh, bank=bank: s_.activation(
                out=m_srk[:, 0, oh * OHW:(oh + 1) * OHW], in_=bank[:, 0:OHW],
                func=AF.Sigmoid, bias=fconst[:, 4:5], scale=fconst[:, 5:6])),
                reads=("pb3",), writes=("m_srk",))
        for qt in range(4):
            for hp in range(NHI // 8):
                bi = 6 + hp % 2
                bank, bkey = pbank[bi], f"pb{bi}"
                for sub in range(2):
                    hj = hp * 2 + sub
                    hi = qt * (NHI // 4) + hj
                    mm_chain(bank[:, sub * SB_:(sub + 1) * SB_], bkey,
                             [(wckT[:, ci, hi * 128:(hi + 1) * 128], m_xk2[par][:, ci, :])
                              for ci in range(NCI)], ("wckT", f"m_xk2{par}"))
                rp_ = hp % 2
                P.op("scalar", lambda s_, bank=bank, rp_=rp_: s_.activation(
                    out=m_rel[rp_][:, 0:2 * SB_], in_=bank[:, 0:2 * SB_], func=AF.Relu,
                    bias=fconst[:, 4:5], scale=fconst[:, 5:6]),
                    reads=(bkey,), writes=(f"m_rel{rp_}",))
                P.op("vector", lambda v, hp=hp, bank=bank, rp_=rp_, qt=qt: v.tensor_mul(
                    m_kkh[qt % 2].ap().rearrange("p a b -> p (a b)")[:, hp * 2 * SB_:
                                                                    (hp * 2 + 2) * SB_],
                    bank[:, 0:2 * SB_], m_rel[rp_][:, 0:2 * SB_]),
                    reads=(bkey, f"m_rel{rp_}"), writes=(f"m_kkh{qt % 2}",))
            for ch in range(NOH):
                bank, bkey = pbank[4 + ch], f"pb{4 + ch}"
                mm_chain(bank[:, 0:OHW], bkey,
                         [(m_kkh[qt % 2][:, hj, 0:128], wcvT[:, qt * (NHI // 4) + hj,
                                                            ch * OHW:(ch + 1) * OHW])
                          for hj in range(NHI // 4)],
                         (f"m_kkh{qt % 2}", "wcvT"), start=(qt == 0), stop=(qt == 3))
        for ch in range(NOH):
            bank, bkey = pbank[4 + ch], f"pb{4 + ch}"
            P.op("vector", lp(lambda v, ch=ch, bank=bank: v.tensor_mul(
                m_rkv[:, 0, ch * OHW:(ch + 1) * OHW], bank[:, 0:OHW],
                m_srk[:, 0, ch * OHW:(ch + 1) * OHW])),
                reads=(bkey, "m_srk"), writes=("m_rkv",))
        P.op("vector", lambda v, par=par: v.tensor_add(
            m_x2[par][:, 0, :], m_x2[par][:, 0, :], m_rkv[:, 0, :]),
            reads=(f"m_x2{par}", "m_rkv"), writes=(f"m_x2{par}",))
        P.dma("sync", "qsync",
              s_hbm[seg0:seg0 + SB_, :].rearrange("(a p) c -> p a c", p=128),
              m_x2[par][:], reads=(f"m_x2{par}",))

    a2_stage1(0)
    a2_stage2(0)
    for s in range(1, g.NSB):
        a2_stage1(s)
        a2_stage3(s - 1)
        a2_stage2(s)
    a2_stage3(g.NSB - 1)
    P.barrier()

    P.dma("gpsimd", "qgps", f_wr[:], wrouter_in.rearrange("(a p) e -> p a e", p=128),
          writes=("f_wr",))
    P.dma("gpsimd", "qgps", f_wgt[:], wgating_in.rearrange("(a p) e -> p a e", p=128),
          writes=("f_wgt",))

    def router_ts2(ts):
        par = 0
        t0 = ts * TS
        P.dma("sync", "qsync", f_xnT[par][:],
              xnT_hbm.rearrange("(a p) t -> p a t", p=128)[:, :, t0:t0 + TS],
              writes=(f"f_xnT{par}",))
        for tt in range(NTT):
            gtt = ts * NTT + tt
            bank = pbank[6]
            mm_chain(bank[:, 0:E], "pb6",
                     [(f_xnT[par][:, ci, tt * 128:(tt + 1) * 128], f_wr[:, ci, :])
                      for ci in range(NCI)], (f"f_xnT{par}", "f_wr"))
            mm_chain(bank[:, 8:8 + E], "pb6b",
                     [(f_xnT[par][:, ci, tt * 128:(tt + 1) * 128], f_wgt[:, ci, :])
                      for ci in range(NCI)], (f"f_xnT{par}", "f_wgt"))
            rp = f_hact[:, 0, 0:E]
            gw = f_hact[:, 0, E:2 * E]
            for dst, dk, off, scale in [(rp, "f_rp", 0, None), (gw, "f_gw", 8, "ten")]:
                P.op("vector", lambda v, off=off: v.reduce_max(
                    f_smx[:, 0:1], bank[:, off:off + E], axis=mybir.AxisListType.X),
                    reads=("pb6", "pb6b"), writes=("f_smx",))
                P.op("vector", lambda v, dst=dst, off=off: v.tensor_single_scalar(
                    dst, bank[:, off:off + E], f_smx[:, 0:1], OP.subtract),
                    reads=("pb6", "pb6b", "f_smx"), writes=(dk,))
                if scale is None:
                    P.op("scalar", lambda s_, dst=dst: s_.activation(
                        out=dst, in_=dst, func=AF.Exp, bias=fconst[:, 4:5], scale=fconst[:, 5:6]), reads=(dk,), writes=(dk,))
                else:
                    P.op("scalar", lambda s_, dst=dst: s_.activation(
                        out=dst, in_=dst, func=AF.Exp, scale=fconst[:, 3:4], bias=fconst[:, 4:5]),
                        reads=(dk, "fconst"), writes=(dk,))
                P.op("vector", lambda v, dst=dst: v.reduce_sum(
                    f_smx[:, 1:2], dst, axis=mybir.AxisListType.X),
                    reads=(dk,), writes=("f_smx2",))
                P.op("vector", lambda v: v.reciprocal(f_smx[:, 2:3], f_smx[:, 1:2]),
                     reads=("f_smx2",), writes=("f_smx3",))
                P.op("vector", lambda v, dst=dst: v.tensor_single_scalar(
                    dst, dst, f_smx[:, 2:3], OP.mult), reads=(dk, "f_smx3"), writes=(dk,))
            P.op("vector", lambda v, gtt=gtt: v.tensor_mul(f_cw[:, gtt, :], rp, gw),
                 reads=("f_rp", "f_gw"), writes=("f_cw",))
            P.op("vector", lambda v, gtt=gtt: v.reduce_sum(
                f_smx[:, 1:2], f_cw[:, gtt, :], axis=mybir.AxisListType.X),
                reads=("f_cw",), writes=("f_smx2",))
            P.op("vector", lambda v: v.tensor_scalar_add(f_smx[:, 1:2], f_smx[:, 1:2], 1e-9),
                 reads=("f_smx2",), writes=("f_smx2",))
            P.op("vector", lambda v: v.reciprocal(f_smx[:, 2:3], f_smx[:, 1:2]),
                 reads=("f_smx2",), writes=("f_smx3",))
            P.op("vector", lambda v, gtt=gtt: v.tensor_single_scalar(
                f_cw[:, gtt, :], f_cw[:, gtt, :], f_smx[:, 2:3], OP.mult),
                reads=("f_cw", "f_smx3"), writes=("f_cw",))

    for ts in range(g.NTS):
        router_ts2(ts)

    wd_ctr = [0]
    def expert_ts(ts):
        par = 0
        t0 = ts * TS
        P.dma("sync", "qsync", f_xnT[par][:],
              xnT_hbm.rearrange("(a p) t -> p a t", p=128)[:, :, t0:t0 + TS],
              writes=(f"f_xnT{par}",))
        for e in range(E):
            for fc in range(g.F // FCH):
                wpar = fc % 2
                P.dma("gpsimd", "qgps", f_wg[wpar][:],
                      wgate_in[e, :, fc * FCH:(fc + 1) * FCH]
                      .rearrange("(a p) f -> p a f", p=128), writes=(f"f_wg{wpar}",))
                P.dma("gpsimd", "qgps", f_wu[wpar][:],
                      wup_in[e, :, fc * FCH:(fc + 1) * FCH]
                      .rearrange("(a p) f -> p a f", p=128), writes=(f"f_wu{wpar}",))
                THW = min(512, TS)
                for fj in range(FCH // 128):
                    fi = fc * (FCH // 128) + fj
                    for th in range(TS // THW):
                        ii = fi * (TS // THW) + th
                        bgi, bui = (0, 1, 2)[ii % 3], (3, 6, 7)[ii % 3]
                        bg, bgkey = pbank[bgi], f"pb{bgi}"
                        bu, bukey = pbank[bui], f"pb{bui}"
                        def gu_fn(t, fj=fj, th=th, wpar=wpar, par=par, bg=bg, bu=bu):
                            for ci in range(NCI):
                                t.matmul(bg[:, 0:THW],
                                         f_wg[wpar][:, ci, fj * 128:(fj + 1) * 128],
                                         f_xnT[par][:, ci, th * THW:(th + 1) * THW],
                                         start=(ci == 0), stop=(ci == NCI - 1))
                            ins = None
                            for ci in range(NCI):
                                ins = t.matmul(bu[:, 0:THW],
                                               f_wu[wpar][:, ci, fj * 128:(fj + 1) * 128],
                                               f_xnT[par][:, ci, th * THW:(th + 1) * THW],
                                               start=(ci == 0), stop=(ci == NCI - 1))
                            return ins
                        P.op("tensor", gu_fn,
                             reads=(f"f_wg{wpar}", f"f_wu{wpar}", f"f_xnT{par}"),
                             writes=(bgkey, bukey))
                        sp = ii % 2
                        P.op("scalar", lambda s_, bg=bg, sp=sp: s_.activation(
                            out=f_sil[sp][:, 0:THW], in_=bg[:, 0:THW], func=AF.Silu,
                            bias=fconst[:, 4:5], scale=fconst[:, 5:6]),
                            reads=(bgkey,), writes=(f"f_sil{sp}",))
                        P.op("vector", lambda v, fi=fi, th=th, bu=bu, sp=sp: v.tensor_mul(
                            f_hact[:, fi, th * THW:(th + 1) * THW], f_sil[sp][:, 0:THW],
                            bu[:, 0:THW]),
                            reads=(f"f_sil{sp}", bukey), writes=("f_hact",))
            NQ = NFI // 4
            chunks = [(hh, q) for hh in range(NOH) for q in range(4 // NOH * 2)][: 8]
            chunks = [(hh, q) for hh in range(NOH) for q in range(4)] if NOH == 2 else \
                     [(0, q) for q in range(4)]
            wd_slots = {}
            def emit_wd(idx):
                hh, q = chunks[idx]
                wi = wd_ctr[0] % 6
                wd_ctr[0] += 1
                wd_slots[(hh, q)] = wi
                P.dma("gpsimd", "qgps", f_wd[wi][:],
                      wdown_in[e, q * (g.F // 4):(q + 1) * (g.F // 4),
                               hh * OHW:(hh + 1) * OHW]
                      .rearrange("(a p) h -> p a h", p=128), writes=(f"f_wd{wi}",))
            nch = len(chunks)
            for idx in range(min(6, nch)):
                emit_wd(idx)
            for hh in range(NOH):
                for tt in range(NTT):
                    bdi = 4 + tt % 2
                    bank, bkey = pbank[bdi], f"pb{bdi}"
                    ws = [wd_slots[(hh, q)] for q in range(4)]
                    pa, pb_ = [], []
                    for q in range(4):
                        lst = pa if q < 2 else pb_
                        lst += [(f_hact[:, q * NQ + j, tt * 128:(tt + 1) * 128],
                                 f_wd[ws[q]][:, j, :]) for j in range(NQ)]
                    mm_chain(bank[:, 0:OHW], bkey, pa,
                             ("f_hact", f"f_wd{ws[0]}", f"f_wd{ws[1]}"),
                             start=True, stop=False)
                    if hh == 0 and tt == NTT - 1 and nch > 6:
                        for idx in range(6, nch):
                            emit_wd(idx)
                    mm_chain(bank[:, 0:OHW], bkey, pb_,
                             ("f_hact", f"f_wd{ws[2]}", f"f_wd{ws[3]}"),
                             start=False, stop=True)
                    gtt = ts * NTT + tt
                    if e == 0:
                        P.op("vector", lambda v, tt=tt, hh=hh, bank=bank, gtt=gtt, e=e:
                             v.tensor_single_scalar(
                                 f_acc[:, tt, hh * OHW:(hh + 1) * OHW], bank[:, 0:OHW],
                                 f_cw[:, gtt, e:e + 1], OP.mult),
                             reads=(bkey, "f_cw"), writes=("f_acc",))
                    else:
                        P.op("vector", lambda v, tt=tt, hh=hh, bank=bank, gtt=gtt, e=e:
                             v.scalar_tensor_tensor(
                                 f_acc[:, tt, hh * OHW:(hh + 1) * OHW], bank[:, 0:OHW],
                                 f_cw[:, gtt, e:e + 1],
                                 f_acc[:, tt, hh * OHW:(hh + 1) * OHW], OP.mult, OP.add),
                             reads=(bkey, "f_cw", "f_acc"), writes=("f_acc",))
        for tt in range(NTT):
            par2 = tt % 2
            r0 = t0 + tt * 128
            P.dma("sync", "qsync", f_x[par2][:], x_own[r0:r0 + 128, :],
                  writes=(f"f_x{par2}",))
            P.dma("sync", "qsync", f_s[par2][:], s_hbm[r0:r0 + 128, :],
                  writes=(f"f_s{par2}",))
            P.op("scalar", lambda s_, tt=tt, par2=par2: s_.activation(
                out=f_g[par2][:, :], in_=f_acc[:, tt, :], func=AF.Silu,
                bias=fconst[:, 4:5], scale=fconst[:, 5:6]),
                reads=("f_acc",), writes=(f"f_g{par2}",))
            P.op("vector", lambda v, par2=par2: v.tensor_mul(
                f_g[par2][:, :], f_g[par2][:, :], f_s[par2][:, :]),
                reads=(f"f_g{par2}", f"f_s{par2}"), writes=(f"f_g{par2}",))
            P.op("vector", lambda v, par2=par2: v.tensor_add(
                f_o[par2][:, :], f_g[par2][:, :], f_x[par2][:, :]),
                reads=(f"f_g{par2}", f"f_x{par2}"), writes=("f_o0",))
            P.dma("sync", "qsync", out_hbm[r0:r0 + 128, :], f_o[par2][:],
                  reads=("f_o0",))

    for ts in range(g.NTS):
        expert_ts(ts)

    P.finish()
    P.emit()
    return nc




from concourse.bass_utils import run_bass_kernel_spmd

_GEO = Geo()


def kernel(**inputs):
    g = _GEO
    T2 = g.T2
    shared = dict(
        mask0=np.zeros((128, 1), np.float32),
        mask1=np.ones((128, 1), np.float32),
        iota=np.tile(np.arange(max(g.SEG1, 128), dtype=np.float32), (128, 1)),
        identf=np.eye(128, dtype=np.float32),
        onescol=np.ones((128, 1), np.float32),
        onesrow=np.ones((1, 128), np.float32),
        fconst=np.tile(np.array([1e-5, -0.5, -1.0, 10.0, 0.0, 1.0], np.float32),
                       (128, 1)),
    )

    def v128(a):
        return np.ascontiguousarray(
            np.asarray(a).reshape(g.NCI, 128).T.astype(np.float32))

    wshared = dict(
        v_ln1g=v128(inputs["L0_attn_ln1_g"]), v_ln1b=v128(inputs["L0_attn_ln1_b"]),
        v_ln2g=v128(inputs["L0_attn_ln2_g"]), v_ln2b=v128(inputs["L0_attn_ln2_b"]),
        v_td=v128(inputs["L0_attn_time_decay"]),
        v_tf=v128(inputs["L0_attn_time_first"]),
        v_tmk=v128(inputs["L0_attn_tm_k"]), v_tmv=v128(inputs["L0_attn_tm_v"]),
        v_tmr=v128(inputs["L0_attn_tm_r"]),
        v_cmk=v128(inputs["L0_attn_cm_k"]), v_cmr=v128(inputs["L0_attn_cm_r"]),
        wkT=np.ascontiguousarray(inputs["L0_attn_Wk"].T),
        wvT=np.ascontiguousarray(inputs["L0_attn_Wv"].T),
        wrT=np.ascontiguousarray(inputs["L0_attn_Wr"].T),
        woT=np.ascontiguousarray(inputs["L0_attn_Wo"].T),
        wckT=np.ascontiguousarray(inputs["L0_attn_Wck"].T),
        wcvT=np.ascontiguousarray(inputs["L0_attn_Wcv"].T),
        wcrT=np.ascontiguousarray(inputs["L0_attn_Wcr"].T),
        wrouter=np.ascontiguousarray(inputs["L0_ffn_w_router"]),
        wgating=np.ascontiguousarray(inputs["L0_ffn_w_gating"]),
        wgate=np.ascontiguousarray(inputs["L0_ffn_w_gate"]),
        wup=np.ascontiguousarray(inputs["L0_ffn_w_up"]),
        wdown=np.ascontiguousarray(inputs["L0_ffn_w_down"]),
    )

    x = np.asarray(inputs["L0_attn_x"], np.float32)
    B = x.shape[0]
    prefs = [np.ascontiguousarray(x[b, 0:T2, :]) for b in range(B)]

    nc = build_nc(g)
    in_maps = []
    for b in range(B):
        for h in range(2):
            m = dict(shared)
            m["mask"] = shared["mask1"] if h else shared["mask0"]
            m.pop("mask0"), m.pop("mask1")
            m.update(wshared)
            m["x_own"] = np.ascontiguousarray(x[b, h * T2:(h + 1) * T2, :])
            m["x_pref"] = prefs[b]
            in_maps.append(m)

    res = run_bass_kernel_spmd(nc, in_maps, core_ids=list(range(8)))
    out = np.empty_like(x)
    for b in range(B):
        for h in range(2):
            out[b, h * T2:(h + 1) * T2, :] = np.asarray(res.results[b * 2 + h]["out"])
    return out


# version 18
# speedup vs baseline: 1.0617x; 1.0617x over previous
import numpy as np
import concourse.bass as bass
import concourse.mybir as mybir

F32 = mybir.dt.float32
BF16 = mybir.dt.bfloat16
AF = mybir.ActivationFunctionType
OP = mybir.AluOpType

EPS = 1e-5


class Geo:
    def __init__(self, C=1024, HID=4096, E=8, F=4096, T2=4096, SEG1=256, SEGA=256,
                 SEGB=128, TSUB=1024, FCH=128):
        self.C, self.HID, self.E, self.F, self.T2 = C, HID, E, F, T2
        self.SEG1, self.SEGA, self.SEGB, self.TSUB, self.FCH = SEG1, SEGA, SEGB, TSUB, FCH
        self.NCI = C // 128
        self.NHI = HID // 128
        self.NFI = F // 128
        self.NS1 = T2 // SEG1
        self.NSA = T2 // SEGA
        self.NSB = T2 // SEGB
        self.NTS = T2 // TSUB


class Prog:
    CENG = ["tensor", "vector", "scalar", "gpsimd", "sync"]

    def __init__(self, nc):
        self.nc = nc
        self.sems, self.counts, self.step = {}, {}, {}
        for e in self.CENG:
            self.sems[e] = nc.alloc_semaphore(f"S_{e}")
            self.counts[e] = 0
            self.step[e] = 1
        self.ops = {e: [] for e in self.CENG}
        self.observed = {e: {} for e in self.CENG}
        self.mem = {}

    def dma_sem(self, name):
        key = "D_" + name
        if key not in self.sems:
            self.sems[key] = self.nc.alloc_semaphore(key)
            self.counts[key] = 0
            self.step[key] = 16
        return key

    def _wait(self, eng, sem_name, cnt):
        if cnt <= 0 or self.observed[eng].get(sem_name, 0) >= cnt:
            return
        self.ops[eng].append(("wait", sem_name, cnt * self.step[sem_name]))
        self.observed[eng][sem_name] = cnt

    def op(self, eng, fn, reads=(), writes=(), queue=None):
        sem_name = queue if queue else eng
        for k in reads:
            st = self.mem.get(k)
            if st:
                for s_, c_ in st["w"].items():
                    self._wait(eng, s_, c_)
        for k in writes:
            st = self.mem.get(k)
            if st:
                for s_, c_ in st["w"].items():
                    self._wait(eng, s_, c_)
                for s_, c_ in st["r"].items():
                    self._wait(eng, s_, c_)
        self.counts[sem_name] += 1
        cnt = self.counts[sem_name]
        self.ops[eng].append(("call", fn, sem_name, cnt))
        for k in writes:
            st = self.mem.setdefault(k, {"w": {}, "r": {}})
            st["w"] = {sem_name: cnt}
            st["r"] = {}
        for k in reads:
            st = self.mem.setdefault(k, {"w": {}, "r": {}})
            st["r"][sem_name] = cnt

    def dma(self, eng, queue, outap, inap, reads=(), writes=()):
        name = writes[0] if writes else "st_" + reads[0]
        self.op(eng, lambda e: e.dma_start(out=outap, in_=inap),
                reads=reads, writes=writes, queue=self.dma_sem(name))

    def barrier(self):
        for e in self.CENG:
            for s_, c_ in self.counts.items():
                self._wait(e, s_, c_)
        self.mem.clear()

    def finish(self):
        for s_, c_ in self.counts.items():
            self._wait("sync", s_, c_)

    def n_items(self):
        return sum(len(v) for v in self.ops.values())

    def emit(self):
        nc = self.nc
        with nc.Block() as block:
            def replay(eng_name):
                def run(eng_obj):
                    for item in self.ops[eng_name]:
                        if item[0] == "wait":
                            _, s_, v = item
                            eng_obj.wait_ge(self.sems[s_], v)
                        else:
                            _, fn, s_, cnt = item
                            ins = fn(eng_obj)
                            ins.then_inc(self.sems[s_], self.step[s_])
                return run
            block.tensor(replay("tensor"))
            block.vector(replay("vector"))
            block.scalar(replay("scalar"))
            block.gpsimd(replay("gpsimd"))
            block.sync(replay("sync"))


class Alloc:
    def __init__(self, nc, budget):
        self.nc = nc
        self.budget = budget

    def region(self, base):
        return Region(self, base)


class Region:
    def __init__(self, alloc, base):
        self.alloc = alloc
        self.cur = base

    def t(self, name, shape, dt):
        nbytes = int(np.prod(shape[1:])) * (2 if dt == BF16 else 4)
        nbytes = (nbytes + 31) & ~31
        h = self.alloc.nc.alloc_sbuf_tensor_at(name, shape, dt, offset=self.cur)
        self.cur += nbytes
        assert self.cur <= self.alloc.budget, f"SBUF overflow at {name}: {self.cur}"
        return h


def build_nc(g: Geo, taps=()):
    nc = bass.Bass("TRN2", target_bir_lowering=False)
    C, NCI, NHI, NFI, T2, E = g.C, g.NCI, g.NHI, g.NFI, g.T2, g.E
    OHW = min(512, C)
    NOH = C // OHW
    taps = set(taps)
    tap_t = {}

    def din(name, shape, dt=F32):
        return nc.dram_tensor(name, shape, dt, kind="ExternalInput").ap()

    x_own = din("x_own", [T2, C])
    x_pref = din("x_pref", [T2, C])
    mask_in = din("mask", [128, 1])
    iota_in = din("iota", [128, max(g.SEG1, 128)])
    identf_in = din("identf", [128, 128])
    onescol_in = din("onescol", [128, 1])
    onesrow_in = din("onesrow", [1, 128])
    fconst_in = din("fconst", [128, 6])
    VEC = ["ln1g", "ln1b", "ln2g", "ln2b", "td", "tf", "tmk", "tmv", "tmr", "cmk", "cmr"]
    vec_in = {n: din("v_" + n, [128, NCI]) for n in VEC}
    wkT_in = din("wkT", [C, C])
    wvT_in = din("wvT", [C, C])
    wrT_in = din("wrT", [C, C])
    woT_in = din("woT", [C, C])
    wckT_in = din("wckT", [C, g.HID])
    wcvT_in = din("wcvT", [g.HID, C])
    wcrT_in = din("wcrT", [C, C])
    wrouter_in = din("wrouter", [C, E])
    wgating_in = din("wgating", [C, E])
    wgate_in = din("wgate", [E, C, g.F])
    wup_in = din("wup", [E, C, g.F])
    wdown_in = din("wdown", [E, g.F, C])

    out_hbm = nc.dram_tensor("out", [T2, C], F32, kind="ExternalOutput").ap()

    xn_hbm = nc.dram_tensor("xn_s", [T2, C], F32, kind="Internal").ap()
    xnT_hbm = nc.dram_tensor("xnT_s", [C, T2], BF16, kind="Internal").ap()
    rwkvT_hbm = nc.dram_tensor("rwkvT_s", [C, T2], BF16, kind="Internal").ap()
    s_hbm = nc.dram_tensor("s_s", [T2, C], F32, kind="Internal").ap()

    def tp(name, shape, dt=F32):
        if name in taps:
            tap_t[name] = nc.dram_tensor("tap_" + name, shape, dt,
                                         kind="ExternalOutput").ap()
            return True
        return False

    SBUF_BASE = 16384
    SBUF_BUDGET = SBUF_BASE + 206 * 1024
    al = Alloc(nc, SBUF_BUDGET)
    R0 = al.region(SBUF_BASE)
    identf = R0.t("identf", [128, 128], F32)
    mask = R0.t("mask", [128, 1], F32)
    iota = R0.t("iota", [128, max(g.SEG1, 128)], F32)
    onescol = R0.t("onescol", [128, 1], F32)
    onesrow = R0.t("onesrow", [1, 128], F32)
    fconst = R0.t("fconst", [128, 6], F32)
    vec = {n: R0.t("v_" + n, [128, NCI], F32) for n in VEC}
    wpos = R0.t("wpos", [128, NCI], F32)
    wneg = R0.t("wneg", [128, NCI], F32)
    dd = R0.t("dd", [128, NCI], F32)
    eu = R0.t("eu", [128, NCI], F32)
    cA = R0.t("cA", [128, NCI], F32)
    cB = R0.t("cB", [128, NCI], F32)
    bdA = R0.t("bdA", [128, NCI], F32)
    bdB = R0.t("bdB", [128, NCI], F32)
    bd_pA = R0.t("bd_pA", [128, NCI], F32)
    bd_pB = R0.t("bd_pB", [128, NCI], F32)
    wbias = R0.t("wbias", [128, NCI], F32)
    h_prevb = R0.t("h_prevb", [128, NCI], BF16)
    h_prev2b = R0.t("h_prev2b", [128, NCI], BF16)
    h2_prevb = R0.t("h2_prevb", [128, NCI], BF16)
    xnbd = R0.t("xnbd", [128, NCI], F32)
    g2m = R0.t("g2m", [128, NCI], F32)
    b2m = R0.t("b2m", [128, NCI], F32)
    bsc = R0.t("bsc", [128, 8 * NCI], F32)
    bscb = R0.t("bscb", [128, 2 * NCI], BF16)
    bsum = R0.t("bsum", [1, 8], F32)
    R0_END = R0.cur
    R1 = al.region(R0_END)
    woT = R1.t("woT", [128, NCI, C], BF16)
    WO_END = R1.cur
    wkT = R1.t("wkT", [128, NCI, C], BF16)
    wvT = R1.t("wvT", [128, NCI, C], BF16)
    wrT = R1.t("wrT", [128, NCI, C], BF16)
    R1_END = R1.cur

    S1 = g.SEG1
    TT1 = S1 // 128
    RA = al.region(R1_END)
    xb = [RA.t(f"xb{i}", [128, TT1, C], F32) for i in range(2)]
    xn_t = RA.t("xn_t", [128, TT1, C], F32)
    stats = RA.t("stats", [128, TT1, 6 * max(1, C // 512)], F32)
    mv = RA.t("mv", [128, TT1, 2], F32)
    rstd = RA.t("rstd", [128, TT1], F32)
    mv2 = RA.t("mv2", [128, TT1, 2], F32)
    rstd2 = RA.t("rstd2", [128, TT1], F32)
    hT = RA.t("hT", [128, NCI, S1 + 1], BF16)
    dmix = RA.t("dmix", [128, NCI, S1], BF16)
    xkT = [RA.t(f"xkT{i}", [128, NCI, S1], BF16) for i in range(2)]
    xvT = [RA.t(f"xvT{i}", [128, NCI, S1], BF16) for i in range(2)]
    xrT = [RA.t(f"xrT{i}", [128, NCI, S1], BF16) for i in range(2)]
    pA = [RA.t(f"pA{i}", [128, NCI, S1], F32) for i in range(2)]
    pB = [RA.t(f"pB{i}", [128, NCI, S1], F32) for i in range(2)]
    srT = [RA.t(f"srT{i}", [128, NCI, S1], BF16) for i in range(2)]
    scanA = RA.t("scanA", [128, NCI, S1 + 1], F32)
    scanB = RA.t("scanB", [128, NCI, S1 + 1], F32)
    gpscr = RA.t("gpscr", [128, S1], BF16)
    numer = RA.t("numer", [128, S1], F32)
    denom = RA.t("denom", [128, S1], F32)
    dinv = RA.t("dinv", [128, S1], F32)
    ybuf = RA.t("ybuf", [128, S1], F32)
    rwkvT = [RA.t(f"rwkvT{i}", [128, NCI, S1], BF16) for i in range(2)]
    xnT_sb = [RA.t(f"xnTsb{i}", [128, NCI, S1], BF16) for i in range(2)]
    cacc = RA.t("cacc", [128, NCI, 2, g.NS1], F32)
    prodscr = [RA.t(f"prodscr{i}", [128, S1], F32) for i in range(2)]
    expw = [RA.t(f"expw{i}", [128, S1], F32) for i in range(2)]
    RA_END = RA.cur

    SB_ = g.SEGB
    R2 = al.region(WO_END)
    wckT = R2.t("wckT", [128, NCI, g.HID], BF16)
    wcvT = R2.t("wcvT", [128, NHI, C], BF16)
    wcrT = R2.t("wcrT", [128, NCI, C], BF16)
    m_xn = [R2.t(f"m_xn{i}", [128, SB_ // 128, C], F32) for i in range(2)]
    m_rw = [R2.t(f"m_rw{i}", [128, NCI, SB_], BF16) for i in range(2)]
    m_x2 = [R2.t(f"m_x2{i}", [128, SB_ // 128, C], F32) for i in range(2)]
    m_stats = R2.t("m_stats", [128, SB_ // 128, 6 * max(1, C // 512)], F32)
    m_mv = R2.t("m_mv", [128, SB_ // 128, 2], F32)
    m_rstd = R2.t("m_rstd", [128, SB_ // 128], F32)
    m_h2T = R2.t("m_h2T", [128, NCI, SB_ + 1], BF16)
    m_dmix = R2.t("m_dmix", [128, NCI, SB_], BF16)
    m_xk2 = [R2.t(f"m_xk2{i}", [128, NCI, SB_], BF16) for i in range(2)]
    m_xr2 = [R2.t(f"m_xr2{i}", [128, NCI, SB_], BF16) for i in range(2)]
    m_kkh = [R2.t(f"m_kkh{i}", [128, NHI // 4, SB_], BF16) for i in range(2)]
    m_rel = [R2.t(f"m_rel{i}", [128, 2 * SB_], BF16) for i in range(2)]
    m_srk = R2.t("m_srk", [128, SB_ // 128, C], BF16)
    m_rkv = R2.t("m_rkv", [128, SB_ // 128, C], BF16)
    assert R2.cur <= al.budget, f"A2 region overflow {R2.cur}"

    TS = g.TSUB
    NTT = TS // 128
    FCH = g.FCH
    R3 = al.region(R0_END)
    f_xnT = [R3.t("f_xnT0", [128, NCI, TS], BF16)] * 2
    f_hact = R3.t("f_hact", [128, NFI, TS], BF16)
    f_acc = R3.t("f_acc", [128, NTT, C], F32)
    f_wg = [R3.t(f"f_wg{i}", [128, NCI, FCH], BF16) for i in range(2)]
    f_wu = [R3.t(f"f_wu{i}", [128, NCI, FCH], BF16) for i in range(2)]
    f_wd = [R3.t(f"f_wd{i}", [128, NFI // 4, OHW], BF16) for i in range(6)]
    f_sil = [R3.t(f"f_sil{i}", [128, 512], BF16) for i in range(2)]
    f_cw = R3.t("f_cw", [128, (T2 // 128), E], F32)
    f_smx = R3.t("f_smx", [128, 4], F32)
    f_wr = R3.t("f_wr", [128, NCI, E], BF16)
    f_wgt = R3.t("f_wgt", [128, NCI, E], BF16)
    f_x = [R3.t(f"f_x{i}", [128, C], F32) for i in range(2)]
    f_s = [R3.t(f"f_s{i}", [128, C], F32) for i in range(2)]
    f_g = [R3.t(f"f_g{i}", [128, C], F32) for i in range(2)]
    f_o = [R3.t("f_o0", [128, C], F32)] * 2
    assert R3.cur <= al.budget, f"F region overflow {R3.cur}"

    pbank = [nc.alloc_psum_tensor(f"pb{i}", [128, 512], F32) for i in range(8)]

    P = Prog(nc)

    for t, src, key in ([(identf, identf_in, "identf"), (mask, mask_in, "mask"),
                         (iota, iota_in, "iota"), (onescol, onescol_in, "onescol"),
                         (onesrow, onesrow_in, "onesrow"), (fconst, fconst_in, "fconst")] +
                        [(vec[n], vec_in[n], "v_" + n) for n in VEC]):
        P.dma("sync", "qsync", t[:], src, writes=(key,))
    for t, src, key in [(wkT, wkT_in, "wkT"), (wvT, wvT_in, "wvT"),
                        (wrT, wrT_in, "wrT"), (woT, woT_in, "woT")]:
        P.dma("gpsimd", "qgps", t[:], src.rearrange("(a p) o -> p a o", p=128),
              writes=(key,))
    P.barrier()

    P.op("scalar", lambda s: s.activation(out=wpos[:], in_=vec["td"][:], func=AF.Exp, bias=fconst[:, 4:5], scale=fconst[:, 5:6]),
         reads=("v_td",), writes=("wpos",))
    P.op("vector", lambda v: v.tensor_scalar_mul(wneg[:], wpos[:], -1.0),
         reads=("wpos",), writes=("wneg",))
    P.op("scalar", lambda s: s.activation(out=dd[:], in_=wpos[:], func=AF.Exp,
                                          scale=fconst[:, 2:3], bias=fconst[:, 4:5]),
         reads=("wpos", "fconst"), writes=("dd",))
    P.op("scalar", lambda s: s.activation(out=eu[:], in_=vec["tf"][:], func=AF.Exp, bias=fconst[:, 4:5], scale=fconst[:, 5:6]),
         reads=("v_tf",), writes=("eu",))
    P.op("vector", lambda v: v.memset(hT[:, :, 0:1], 0.0), writes=("hT_c0",))
    P.op("vector", lambda v: v.tensor_scalar_mul(g2m[:], vec["ln2g"][:], mask[:, 0:1]),
         reads=("v_ln2g", "mask"), writes=("g2m",))
    P.op("vector", lambda v: v.tensor_scalar_mul(b2m[:], vec["ln2b"][:], mask[:, 0:1]),
         reads=("v_ln2b", "mask"), writes=("b2m",))

    def ln_stats(xkey, xview, statt, statkey, mvten, mvkey, rstdten, rstdkey, tt):
        nch = max(1, C // 512)
        chw = min(512, C)
        for j in range(nch):
            P.op("vector", lambda v, j=j: v.bn_stats(
                statt[:, tt, 6 * j:6 * (j + 1)], xview[:, j * chw:(j + 1) * chw]),
                reads=(xkey,), writes=(statkey,))
        P.op("vector", lambda v: v.bn_aggr(mvten[:, tt, :], statt[:, tt, :]),
             reads=(statkey,), writes=(mvkey,))
        P.op("scalar", lambda s: s.activation(out=rstdten[:, tt:tt + 1],
                                              in_=mvten[:, tt, 1:2], func=AF.Ln,
                                              bias=fconst[:, 0:1], scale=fconst[:, 5:6]),
             reads=(mvkey, "fconst"), writes=(rstdkey + "_t",))
        P.op("scalar", lambda s: s.activation(out=rstdten[:, tt:tt + 1],
                                              in_=rstdten[:, tt:tt + 1], func=AF.Exp,
                                              scale=fconst[:, 1:2], bias=fconst[:, 4:5]),
             reads=(rstdkey + "_t", "fconst"), writes=(rstdkey,))

    def transpose_to(src, srckey, dst, dstkey, ttn, col_off, banks, gb=None, extra_reads=()):
        for ci in range(NCI):
            bi = banks[ci % len(banks)]
            bank, bkey = pbank[bi], f"pb{bi}"
            def tfn(t, ci=ci, bank=bank):
                ins = None
                for tt in range(ttn):
                    ins = t.transpose(bank[:, tt * 128:(tt + 1) * 128],
                                      src[:, tt, ci * 128:(ci + 1) * 128], identf[:])
                return ins
            P.op("tensor", tfn, reads=(srckey, "identf") + tuple(extra_reads), writes=(bkey,))
            if gb is None:
                P.op("scalar", lambda s_, ci=ci, bank=bank: s_.copy(
                    out=dst[:, ci, col_off:col_off + ttn * 128], in_=bank[:, 0:ttn * 128]),
                    reads=(bkey,), writes=(dstkey,))
            else:
                gv, bv, gk, bk2 = gb
                P.op("scalar", lambda s_, ci=ci, bank=bank, gv=gv, bv=bv: s_.activation(
                    out=dst[:, ci, col_off:col_off + ttn * 128], in_=bank[:, 0:ttn * 128],
                    func=AF.Identity, bias=bv[:, ci:ci + 1], scale=gv[:, ci:ci + 1]),
                    reads=(bkey, gk, bk2), writes=(dstkey,))

    def lp(fn):
        def wrapped(e):
            with nc.allow_low_precision(reason="bf16 store on matmul-operand path"):
                return fn(e)
        return wrapped

    def mm_chain(psum_ap, psum_key, pairs, rkeys, start=True, stop=True):
        n = len(pairs)
        def fn(t):
            ins = None
            for i, (l, r) in enumerate(pairs):
                ins = t.matmul(psum_ap, l, r, start=(start and i == 0),
                               stop=(stop and i == n - 1))
            return ins
        P.op("tensor", fn, reads=tuple(rkeys), writes=(psum_key,))

    def a1_load(s, is_prefix):
        par = s % 2
        srcx = x_pref if is_prefix else x_own
        seg0 = s * S1
        P.dma("sync", "qsync", xb[par][:],
              srcx[seg0:seg0 + S1, :].rearrange("(a p) c -> p a c", p=128),
              writes=(f"xb{par}",))

    def a1_stageA(s, is_prefix):
        par = s % 2
        xkey = f"xb{par}"
        seg0 = s * S1
        if s + 1 < g.NS1:
            a1_load(s + 1, is_prefix)

        nchv = max(1, C // 512)
        chw = min(512, C)
        for tt in range(TT1):
            for j in range(nchv):
                P.op("vector", lambda v, tt=tt, j=j, par=par: v.bn_stats(
                    stats[:, tt, 6 * j:6 * (j + 1)], xb[par][:, tt, j * chw:(j + 1) * chw]),
                    reads=(xkey,), writes=("stats",))
            P.op("vector", lambda v, tt=tt: v.bn_aggr(mv[:, tt, :], stats[:, tt, :]),
                 reads=("stats",), writes=("mv",))
        P.op("scalar", lambda s: s.activation(out=rstd[:, :], in_=mv[:, :, 1],
                                              func=AF.Ln, bias=fconst[:, 0:1],
                                              scale=fconst[:, 5:6]),
             reads=("mv", "fconst"), writes=("rstd_t",))
        P.op("scalar", lambda s: s.activation(out=rstd[:, :], in_=rstd[:, :],
                                              func=AF.Exp, scale=fconst[:, 1:2],
                                              bias=fconst[:, 4:5]),
             reads=("rstd_t", "fconst"), writes=("rstd",))
        for tt in range(TT1):
            P.op("vector", lambda v, tt=tt, par=par: v.tensor_scalar(
                xn_t[:, tt, :], xb[par][:, tt, :], mv[:, tt, 0:1], rstd[:, tt:tt + 1],
                OP.subtract, OP.mult),
                reads=(xkey, "mv", "rstd"), writes=("xn_t",))
            for j in range(nchv):
                P.op("vector", lambda v, tt=tt, j=j: v.bn_stats(
                    stats[:, tt, 6 * j:6 * (j + 1)], xn_t[:, tt, j * chw:(j + 1) * chw]),
                    reads=("xn_t",), writes=("stats2",))
            P.op("vector", lambda v, tt=tt: v.bn_aggr(mv2[:, tt, :], stats[:, tt, :]),
                 reads=("stats2",), writes=("mv2",))
        P.op("scalar", lambda s: s.activation(out=rstd2[:, :], in_=mv2[:, :, 1],
                                              func=AF.Ln, bias=fconst[:, 0:1],
                                              scale=fconst[:, 5:6]),
             reads=("mv2", "fconst"), writes=("rstd2_t",))
        P.op("scalar", lambda s: s.activation(out=rstd2[:, :], in_=rstd2[:, :],
                                              func=AF.Exp, scale=fconst[:, 1:2],
                                              bias=fconst[:, 4:5]),
             reads=("rstd2_t", "fconst"), writes=("rstd2",))
        for tt in range(TT1):
            P.op("vector", lambda v, tt=tt, par=par: v.tensor_scalar(
                xb[par][:, tt, :], xn_t[:, tt, :], mv2[:, tt, 0:1], rstd2[:, tt:tt + 1],
                OP.subtract, OP.mult),
                reads=("xn_t", "mv2", "rstd2"), writes=(xkey,))

        if not is_prefix:
            P.dma("sync", "qsync",
                  xn_hbm[seg0:seg0 + S1, :].rearrange("(a p) c -> p a c", p=128),
                  xn_t[:], reads=("xn_t",))
            transpose_to(xn_t, "xn_t", xnT_sb[par], f"xnT{par}", TT1, 0, (0, 1))
            P.dma("sync", "qsync",
                  xnT_hbm.rearrange("(a p) t -> p a t", p=128)[:, :, seg0:seg0 + S1],
                  xnT_sb[par][:], reads=(f"xnT{par}",))

        transpose_to(xb[par], xkey, hT, "hT", TT1, 1, (2, 3),
                     gb=(vec["ln1g"], vec["ln1b"], "v_ln1g", "v_ln1b"))

        P.op("gpsimd", lambda gp: gp.tensor_sub(
            dmix[:, :, :], hT[:, :, 1:S1 + 1], hT[:, :, 0:S1]),
            reads=("hT", "hT_c0"), writes=("dmix",))
        for ci in range(NCI):
            for dst, dk, tmn, eng in ([(xkT[par], f"xkT{par}", "tmk", "vector"),
                                       (xvT[par], f"xvT{par}", "tmv", "gpsimd")] +
                                      ([(xrT[par], f"xrT{par}", "tmr", "vector")]
                                       if not is_prefix else [])):
                if eng == "gpsimd":
                    P.op("gpsimd", lambda gp, ci=ci, tmn=tmn: gp.tensor_mul(
                        gpscr[:, :], dmix[:, ci, :],
                        vec[tmn][:, ci:ci + 1].broadcast_to([128, S1])),
                        reads=("dmix", "v_" + tmn), writes=("gpscr",))
                    P.op("gpsimd", lambda gp, ci=ci, dst=dst: gp.tensor_add(
                        dst[:, ci, :], gpscr[:, :], hT[:, ci, 0:S1]),
                        reads=("gpscr", "hT", "hT_c0"), writes=(dk,))
                else:
                    P.op(eng, lambda v, ci=ci, dst=dst, tmn=tmn: v.scalar_tensor_tensor(
                        dst[:, ci, :], dmix[:, ci, :], vec[tmn][:, ci:ci + 1],
                        hT[:, ci, 0:S1], OP.mult, OP.add),
                        reads=("dmix", "v_" + tmn, "hT", "hT_c0"), writes=(dk,))

        if is_prefix and s == g.NS1 - 1:
            P.op("vector", lambda v: v.tensor_copy(h_prevb[:, :],
                                                   hT[:, :, S1:S1 + 1].squeeze(2)),
                 reads=("hT",), writes=("h_prevb",))
            P.op("vector", lambda v: v.tensor_copy(h_prev2b[:, :],
                                                   hT[:, :, S1 - 1:S1].squeeze(2)),
                 reads=("hT",), writes=("h_prev2b",))
        P.op("vector", lambda v: v.tensor_copy(hT[:, :, 0:1], hT[:, :, S1:S1 + 1]),
             reads=("hT",), writes=("hT_c0",))


    def a1_stageB(s, is_prefix):
        par = s % 2
        seg0 = s * S1
        for oi in range(NCI):
            bki = 4 + (oi % 4)
            bk, bkkey = pbank[bki], f"pb{bki}"
            mm_chain(bk[:, 0:S1], bkkey,
                     [(wkT[:, ci, oi * 128:(oi + 1) * 128], xkT[par][:, ci, :])
                      for ci in range(NCI)], ("wkT", f"xkT{par}"))
            P.op("scalar", lambda s_, oi=oi, bk=bk, par=par: s_.activation(
                out=pB[par][:, oi, :], in_=bk[:, 0:S1], func=AF.Exp, bias=fconst[:, 4:5], scale=fconst[:, 5:6]),
                reads=(bkkey,), writes=(f"pB{par}",))
            bvi = (oi % 4)
            bv, bvkey = pbank[bvi], f"pb{bvi}"
            mm_chain(bv[:, 0:S1], bvkey,
                     [(wvT[:, ci, oi * 128:(oi + 1) * 128], xvT[par][:, ci, :])
                      for ci in range(NCI)], ("wvT", f"xvT{par}"))
            P.op("vector", lambda v, oi=oi, bv=bv, par=par: v.tensor_mul(
                pA[par][:, oi, :], bv[:, 0:S1], pB[par][:, oi, :]),
                reads=(bvkey, f"pB{par}"), writes=(f"pA{par}",))
        for oi in range(NCI):
            if not is_prefix:
                bki = 4 + (oi % 4)
                bk, bkkey = pbank[bki], f"pb{bki}"
                mm_chain(bk[:, 0:S1], bkkey,
                         [(wrT[:, ci, oi * 128:(oi + 1) * 128], xrT[par][:, ci, :])
                          for ci in range(NCI)], ("wrT", f"xrT{par}"))
                P.op("scalar", lambda s_, oi=oi, bk=bk, par=par: s_.activation(
                    out=srT[par][:, oi, :], in_=bk[:, 0:S1], func=AF.Exp, scale=fconst[:, 2:3], bias=fconst[:, 4:5]),
                    reads=(bkkey, "fconst"), writes=(f"srT{par}",))

        if not is_prefix:
            P.op("vector", lp(lambda v, par=par: v.tensor_scalar_add(
                srT[par][:, :, :], srT[par][:, :, :], 1.0)),
                reads=(f"srT{par}",), writes=(f"srT{par}",))
            P.op("vector", lp(lambda v, par=par: v.reciprocal(
                srT[par].ap().rearrange("p a b -> p (a b)"),
                srT[par].ap().rearrange("p a b -> p (a b)"))),
                reads=(f"srT{par}",), writes=(f"srT{par}",))

        if is_prefix:
            FD = S1 - 1 if s == g.NS1 - 1 else S1
            P.op("vector", lambda v, s=s: v.tensor_scalar_mul(
                wbias[:, :], wneg[:, :], float(T2 - 2 - s * S1)),
                reads=("wneg",), writes=("wbias",))
            for oi in range(NCI):
                ep = oi % 2
                P.op("scalar", lambda s_, oi=oi, FD=FD, ep=ep: s_.activation(
                    out=expw[ep][:, 0:FD], in_=iota[:, 0:FD], func=AF.Exp,
                    bias=wbias[:, oi:oi + 1], scale=wpos[:, oi:oi + 1]),
                    reads=("iota", "wbias", "wpos"), writes=(f"expw{ep}",))
                for pten, pk, col in [(pA[par], f"pA{par}", 0), (pB[par], f"pB{par}", 1)]:
                    P.op("vector", lambda v, oi=oi, pten=pten, col=col, FD=FD, ep=ep, s=s:
                         v.scalar_tensor_tensor(
                             prodscr[col][:, 0:FD], expw[ep][:, 0:FD], 1.0,
                             pten[:, oi, 0:FD], OP.mult, OP.mult,
                             accum_out=cacc[:, oi, col, s:s + 1]),
                         reads=(f"expw{ep}", pk), writes=(f"prodscr{col}", "cacc"))
            if s == g.NS1 - 1:
                P.op("vector", lambda v, par=par: v.tensor_copy(
                    bd_pA[:, :], pA[par][:, :, S1 - 1:S1].squeeze(2)),
                     reads=(f"pA{par}",), writes=("bd_pA",))
                P.op("vector", lambda v, par=par: v.tensor_copy(
                    bd_pB[:, :], pB[par][:, :, S1 - 1:S1].squeeze(2)),
                     reads=(f"pB{par}",), writes=("bd_pB",))
                for ci in range(NCI):
                    P.op("tensor", lambda t, ci=ci: t.transpose(
                        pbank[0][:, 0:128], xn_t[:, TT1 - 1, ci * 128:(ci + 1) * 128],
                        identf[:]),
                        reads=("xn_t", "identf"), writes=("pb0",))
                    P.op("vector", lambda v, ci=ci: v.tensor_copy(
                        xnbd[:, ci:ci + 1], pbank[0][:, 127:128]),
                        reads=("pb0",), writes=("xnbd",))
        else:
            P.op("vector", lambda v: v.tensor_copy(
                scanA[:, :, 0:1], cA[:, :].unsqueeze(2)), reads=("cA",), writes=("scanA",))
            P.op("vector", lambda v: v.tensor_copy(
                scanB[:, :, 0:1], cB[:, :].unsqueeze(2)), reads=("cB",), writes=("scanB",))
            for oi in range(NCI):
                P.op("vector", lambda v, oi=oi, par=par: v.tensor_tensor_scan(
                    scanA[:, oi, 1:S1 + 1], dd[:, oi:oi + 1].broadcast_to([128, S1]),
                    pA[par][:, oi, :], cA[:, oi:oi + 1], OP.mult, OP.add),
                    reads=("dd", f"pA{par}", "cA"), writes=("scanA",))
                P.op("vector", lambda v, oi=oi, par=par: v.tensor_tensor_scan(
                    scanB[:, oi, 1:S1 + 1], dd[:, oi:oi + 1].broadcast_to([128, S1]),
                    pB[par][:, oi, :], cB[:, oi:oi + 1], OP.mult, OP.add),
                    reads=("dd", f"pB{par}", "cB"), writes=("scanB",))
                P.op("vector", lambda v, oi=oi, par=par: v.scalar_tensor_tensor(
                    scanA[:, oi, 0:S1], pA[par][:, oi, :], eu[:, oi:oi + 1],
                    scanA[:, oi, 0:S1],
                    OP.mult, OP.add), reads=(f"pA{par}", "eu", "scanA"), writes=("scanA",))
                P.op("vector", lambda v, oi=oi, par=par: v.scalar_tensor_tensor(
                    scanB[:, oi, 0:S1], pB[par][:, oi, :], eu[:, oi:oi + 1],
                    scanB[:, oi, 0:S1],
                    OP.mult, OP.add), reads=(f"pB{par}", "eu", "scanB"), writes=("scanB",))
            P.op("vector", lambda v: v.reciprocal(
                scanB[:, :, 0:S1], scanB[:, :, 0:S1]),
                reads=("scanB",), writes=("scanB",))
            P.op("gpsimd", lambda gp: gp.tensor_mul(
                scanA[:, :, 0:S1], scanA[:, :, 0:S1], scanB[:, :, 0:S1]),
                reads=("scanA", "scanB"), writes=("scanA",))
            P.op("gpsimd", lambda gp, par=par: gp.tensor_mul(
                rwkvT[par][:, :, :], scanA[:, :, 0:S1], srT[par][:, :, :]),
                reads=("scanA", f"srT{par}"), writes=(f"rwkvT{par}",))
            P.op("vector", lambda v: v.tensor_copy(
                cA[:, :], scanA[:, :, S1:S1 + 1].squeeze(2)), reads=("scanA",), writes=("cA",))
            P.op("vector", lambda v: v.tensor_copy(
                cB[:, :], scanB[:, :, S1:S1 + 1].squeeze(2)), reads=("scanB",), writes=("cB",))
            P.dma("sync", "qsync",
                  rwkvT_hbm.rearrange("(a p) t -> p a t", p=128)[:, :, seg0:seg0 + S1],
                  rwkvT[par][:], reads=(f"rwkvT{par}",))

    def a1_boundary():
        t0 = bsc[:, 0:NCI]
        P.op("vector", lambda v: v.tensor_sub(t0, h_prevb[:], h_prev2b[:]),
             reads=("h_prevb", "h_prev2b"), writes=("bt0",))
        P.op("vector", lambda v: v.tensor_mul(t0, t0, vec["tmr"][:]),
             reads=("bt0", "v_tmr"), writes=("bt0",))
        xrbd = bscb[:, 0:NCI]
        P.op("vector", lambda v: v.tensor_add(xrbd, t0, h_prev2b[:]),
             reads=("bt0", "h_prev2b"), writes=("xrbd",))
        for oi in range(NCI):
            mm_chain(pbank[0][:, oi:oi + 1], "pb0",
                     [(wrT[:, ci, oi * 128:(oi + 1) * 128], xrbd[:, ci:ci + 1])
                      for ci in range(NCI)], ("wrT", "xrbd"))
        srbd = bsc[:, NCI:2 * NCI]
        P.op("scalar", lambda s: s.activation(out=srbd, in_=pbank[0][:, 0:NCI],
                                              func=AF.Exp, scale=fconst[:, 2:3], bias=fconst[:, 4:5]),
             reads=("pb0", "fconst"), writes=("srbd",))
        P.op("vector", lambda v: v.tensor_scalar_add(srbd, srbd, 1.0),
             reads=("srbd",), writes=("srbd",))
        P.op("vector", lambda v: v.reciprocal(srbd, srbd), reads=("srbd",), writes=("srbd",))
        ybd = bsc[:, 2 * NCI:3 * NCI]
        dbd = bsc[:, 3 * NCI:4 * NCI]
        for oi in range(NCI):
            P.op("vector", lambda v, oi=oi: v.scalar_tensor_tensor(
                ybd[:, oi:oi + 1], bd_pA[:, oi:oi + 1], eu[:, oi:oi + 1], bdA[:, oi:oi + 1],
                OP.mult, OP.add), reads=("bd_pA", "eu", "bdA"), writes=("ybd",))
            P.op("vector", lambda v, oi=oi: v.scalar_tensor_tensor(
                dbd[:, oi:oi + 1], bd_pB[:, oi:oi + 1], eu[:, oi:oi + 1], bdB[:, oi:oi + 1],
                OP.mult, OP.add), reads=("bd_pB", "eu", "bdB"), writes=("dbd",))
        P.op("vector", lambda v: v.reciprocal(dbd, dbd), reads=("dbd",), writes=("dbd",))
        P.op("vector", lambda v: v.tensor_mul(ybd, ybd, dbd), reads=("ybd", "dbd"),
             writes=("ybd",))
        rwbd = bscb[:, NCI:2 * NCI]
        P.op("vector", lambda v: v.tensor_mul(rwbd, ybd, srbd),
             reads=("ybd", "srbd"), writes=("rwbd",))
        for oi in range(NCI):
            mm_chain(pbank[1][:, oi:oi + 1], "pb1",
                     [(woT[:, ci, oi * 128:(oi + 1) * 128], rwbd[:, ci:ci + 1])
                      for ci in range(NCI)], ("woT", "rwbd"))
        x2bd = bsc[:, 4 * NCI:5 * NCI]
        P.op("vector", lambda v: v.tensor_add(x2bd, xnbd[:], pbank[1][:, 0:NCI]),
             reads=("xnbd", "pb1"), writes=("x2bd",))
        mm_chain(pbank[2][0:1, 0:1], "pb2",
                 [(x2bd[:, ci:ci + 1], onescol[:]) for ci in range(NCI)],
                 ("x2bd", "onescol"))
        mm_chain(pbank[2][0:1, 1:2], "pb2b",
                 [(x2bd[:, ci:ci + 1], x2bd[:, ci:ci + 1]) for ci in range(NCI)],
                 ("x2bd",))
        P.op("vector", lambda v: v.tensor_scalar_mul(bsum[0:1, 0:2], pbank[2][0:1, 0:2],
                                                     1.0 / C),
             reads=("pb2", "pb2b"), writes=("bsum",))
        P.op("vector", lambda v: v.tensor_mul(bsum[0:1, 2:3], bsum[0:1, 0:1],
                                              bsum[0:1, 0:1]),
             reads=("bsum",), writes=("bsum2",))
        P.op("vector", lambda v: v.tensor_sub(bsum[0:1, 3:4], bsum[0:1, 1:2],
                                              bsum[0:1, 2:3]),
             reads=("bsum", "bsum2"), writes=("bvar",))
        P.op("scalar", lambda s: s.activation(out=bsum[0:1, 4:5], in_=bsum[0:1, 3:4],
                                              func=AF.Ln, bias=fconst[0:1, 0:1], scale=fconst[0:1, 5:6]),
             reads=("bvar", "fconst"), writes=("blnv",))
        P.op("scalar", lambda s: s.activation(out=bsum[0:1, 1:2], in_=bsum[0:1, 4:5],
                                              func=AF.Exp, scale=fconst[0:1, 1:2], bias=fconst[0:1, 4:5]),
             reads=("blnv", "fconst"), writes=("brstd",))
        mm_chain(pbank[3][:, 0:2], "pb3", [(onesrow[:], bsum[0:1, 0:2])],
                 ("onesrow", "bsum", "brstd"))
        for ci in range(NCI):
            P.op("vector", lambda v, ci=ci: v.tensor_scalar(
                t0[:, ci:ci + 1], x2bd[:, ci:ci + 1], pbank[3][:, 0:1], pbank[3][:, 1:2],
                OP.subtract, OP.mult),
                reads=("x2bd", "pb3"), writes=("bt0",))
        P.op("vector", lambda v: v.scalar_tensor_tensor(
            h2_prevb[:, :], t0, 1.0, g2m[:], OP.mult, OP.mult),
            reads=("bt0", "g2m"), writes=("h2_prevb_t",))
        P.op("vector", lambda v: v.tensor_add(h2_prevb[:, :], h2_prevb[:, :], b2m[:]),
             reads=("h2_prevb_t", "b2m"), writes=("h2_prevb",))
        for cdst, ckey, bsrc, bk, psrc, pk in [(cA, "cA", bdA, "bdA", bd_pA, "bd_pA"),
                                               (cB, "cB", bdB, "bdB", bd_pB, "bd_pB")]:
            P.op("vector", lambda v, cdst=cdst, bsrc=bsrc, psrc=psrc: v.tensor_mul(
                cdst[:], dd[:], bsrc[:]), reads=("dd", bk), writes=(ckey,))
            P.op("vector", lambda v, cdst=cdst, psrc=psrc: v.tensor_add(
                cdst[:], cdst[:], psrc[:]), reads=(ckey, pk), writes=(ckey,))
            P.op("vector", lambda v, cdst=cdst: v.tensor_scalar_mul(
                cdst[:], cdst[:], mask[:, 0:1]), reads=(ckey, "mask"), writes=(ckey,))
        P.op("vector", lambda v: v.tensor_scalar_mul(
            hT[:, :, 0:1], h_prevb[:, :].unsqueeze(2), mask[:, 0:1]),
            reads=("h_prevb", "mask"), writes=("hT_c0",))

    a1_load(0, True)
    a1_stageA(0, True)
    for s in range(1, g.NS1):
        a1_stageA(s, True)
        a1_stageB(s - 1, True)
    a1_stageB(g.NS1 - 1, True)
    for oi in range(NCI):
        P.op("vector", lambda v, oi=oi: v.reduce_sum(
            bdA[:, oi:oi + 1], cacc[:, oi, 0, :], axis=mybir.AxisListType.X),
            reads=("cacc",), writes=("bdA",))
        P.op("vector", lambda v, oi=oi: v.reduce_sum(
            bdB[:, oi:oi + 1], cacc[:, oi, 1, :], axis=mybir.AxisListType.X),
            reads=("cacc",), writes=("bdB",))
    a1_boundary()
    a1_load(0, False)
    a1_stageA(0, False)
    for s in range(1, g.NS1):
        a1_stageA(s, False)
        a1_stageB(s - 1, False)
    a1_stageB(g.NS1 - 1, False)
    if tp("bdA", [128, NCI]):
        P.dma("sync", "qsync", tap_t["bdA"], bdA[:], reads=("bdA",))
    if tp("cA", [128, NCI]):
        P.dma("sync", "qsync", tap_t["cA"], cA[:], reads=("cA",))
    if tp("h2_prevb", [128, NCI]):
        P.op("vector", lambda v: v.tensor_copy(bsc[:, 0:NCI], h2_prevb[:]),
             reads=("h2_prevb",), writes=("bt0",))
        P.dma("sync", "qsync", tap_t["h2_prevb"], bsc[:, 0:NCI], reads=("bt0",))

    P.barrier()

    for t, srcw, key in [(wckT, wckT_in, "wckT"), (wcvT, wcvT_in, "wcvT"),
                         (wcrT, wcrT_in, "wcrT")]:
        P.dma("gpsimd", "qgps", t[:], srcw.rearrange("(a p) o -> p a o", p=128),
              writes=(key,))

    def a2_load(s):
        par = s % 2
        seg0 = s * SB_
        P.dma("sync", "qsync", m_xn[par][:],
              xn_hbm[seg0:seg0 + SB_, :].rearrange("(a p) c -> p a c", p=128),
              writes=(f"m_xn{par}",))
        P.dma("sync", "qsync", m_rw[par][:],
              rwkvT_hbm.rearrange("(a p) t -> p a t", p=128)[:, :, seg0:seg0 + SB_],
              writes=(f"m_rw{par}",))

    def a2_stage1(s):
        par = s % 2
        seg0 = s * SB_
        if s + 1 < g.NSB:
            a2_load(s + 1)
        for oh in range(NOH):
            bi = (0, 1)[oh % 2]
            bank, bkey = pbank[bi], f"pb{bi}"
            mm_chain(bank[:, 0:OHW], bkey,
                     [(m_rw[par][:, ci, 0:128], woT[:, ci, oh * OHW:(oh + 1) * OHW])
                      for ci in range(NCI)], (f"m_rw{par}", "woT"))
            P.op("vector", lambda v, oh=oh, bank=bank, par=par: v.tensor_add(
                m_x2[par][:, 0, oh * OHW:(oh + 1) * OHW],
                m_xn[par][:, 0, oh * OHW:(oh + 1) * OHW], bank[:, 0:OHW]),
                reads=(f"m_xn{par}", bkey), writes=(f"m_x2{par}",))

    def a2_stage2(s):
        par = s % 2
        seg0 = s * SB_
        nchv = max(1, C // 512)
        chw = min(512, C)
        for j in range(nchv):
            P.op("vector", lambda v, j=j, par=par: v.bn_stats(
                m_stats[:, 0, 6 * j:6 * (j + 1)], m_x2[par][:, 0, j * chw:(j + 1) * chw]),
                reads=(f"m_x2{par}",), writes=("m_stats",))
        P.op("vector", lambda v: v.bn_aggr(m_mv[:, 0, :], m_stats[:, 0, :]),
             reads=("m_stats",), writes=("m_mv",))
        P.op("scalar", lambda s_: s_.activation(out=m_rstd[:, :], in_=m_mv[:, :, 1],
                                                func=AF.Ln, bias=fconst[:, 0:1],
                                                scale=fconst[:, 5:6]),
             reads=("m_mv", "fconst"), writes=("m_rstd_t",))
        P.op("scalar", lambda s_: s_.activation(out=m_rstd[:, :], in_=m_rstd[:, :],
                                                func=AF.Exp, scale=fconst[:, 1:2],
                                                bias=fconst[:, 4:5]),
             reads=("m_rstd_t", "fconst"), writes=("m_rstd",))
        P.op("vector", lambda v, par=par: v.tensor_scalar(
            m_xn[par][:, 0, :], m_x2[par][:, 0, :], m_mv[:, 0, 0:1], m_rstd[:, 0:1],
            OP.subtract, OP.mult),
            reads=(f"m_x2{par}", "m_mv", "m_rstd"), writes=(f"m_xn{par}",))
        if s == 0:
            P.op("vector", lambda v: v.tensor_copy(m_h2T[:, :, 0:1],
                                                   h2_prevb[:, :].unsqueeze(2)),
                 reads=("h2_prevb",), writes=("m_h2T_c0",))
        for ci in range(NCI):
            bi = (2, 3)[ci % 2]
            bank, bkey = pbank[bi], f"pb{bi}"
            P.op("tensor", lambda t, ci=ci, bank=bank, par=par: t.transpose(
                bank[:, 0:128], m_xn[par][:, 0, ci * 128:(ci + 1) * 128], identf[:]),
                reads=(f"m_xn{par}", "identf"), writes=(bkey,))
            P.op("scalar", lambda s_, ci=ci, bank=bank: s_.activation(
                out=m_h2T[:, ci, 1:SB_ + 1], in_=bank[:, 0:128],
                func=AF.Identity, bias=vec["ln2b"][:, ci:ci + 1],
                scale=vec["ln2g"][:, ci:ci + 1]),
                reads=(bkey, "v_ln2g", "v_ln2b"), writes=("m_h2T",))
        P.op("gpsimd", lambda gp: gp.tensor_sub(
            m_dmix[:, :, :], m_h2T[:, :, 1:SB_ + 1], m_h2T[:, :, 0:SB_]),
            reads=("m_h2T", "m_h2T_c0"), writes=("m_dmix",))
        for ci in range(NCI):
            P.op("vector", lambda v, ci=ci, par=par: v.scalar_tensor_tensor(
                m_xk2[par][:, ci, :], m_dmix[:, ci, :], vec["cmk"][:, ci:ci + 1],
                m_h2T[:, ci, 0:SB_], OP.mult, OP.add),
                reads=("m_dmix", "v_cmk", "m_h2T", "m_h2T_c0"), writes=(f"m_xk2{par}",))
            P.op("vector", lambda v, ci=ci, par=par: v.scalar_tensor_tensor(
                m_xr2[par][:, ci, :], m_dmix[:, ci, :], vec["cmr"][:, ci:ci + 1],
                m_h2T[:, ci, 0:SB_], OP.mult, OP.add),
                reads=("m_dmix", "v_cmr", "m_h2T", "m_h2T_c0"), writes=(f"m_xr2{par}",))
        P.op("vector", lambda v: v.tensor_copy(m_h2T[:, :, 0:1], m_h2T[:, :, SB_:SB_ + 1]),
             reads=("m_h2T",), writes=("m_h2T_c0",))

    def a2_stage3(s):
        par = s % 2
        seg0 = s * SB_
        for oh in range(NOH):
            bank, bkey = pbank[3], "pb3"
            mm_chain(bank[:, 0:OHW], bkey,
                     [(m_xr2[par][:, ci, 0:128], wcrT[:, ci, oh * OHW:(oh + 1) * OHW])
                      for ci in range(NCI)], (f"m_xr2{par}", "wcrT"))
            P.op("scalar", lp(lambda s_, oh=oh, bank=bank: s_.activation(
                out=m_srk[:, 0, oh * OHW:(oh + 1) * OHW], in_=bank[:, 0:OHW],
                func=AF.Sigmoid, bias=fconst[:, 4:5], scale=fconst[:, 5:6])),
                reads=("pb3",), writes=("m_srk",))
        for qt in range(4):
            for hp in range(NHI // 8):
                bi = 6 + hp % 2
                bank, bkey = pbank[bi], f"pb{bi}"
                for sub in range(2):
                    hj = hp * 2 + sub
                    hi = qt * (NHI // 4) + hj
                    mm_chain(bank[:, sub * SB_:(sub + 1) * SB_], bkey,
                             [(wckT[:, ci, hi * 128:(hi + 1) * 128], m_xk2[par][:, ci, :])
                              for ci in range(NCI)], ("wckT", f"m_xk2{par}"))
                rp_ = hp % 2
                P.op("scalar", lambda s_, bank=bank, rp_=rp_: s_.activation(
                    out=m_rel[rp_][:, 0:2 * SB_], in_=bank[:, 0:2 * SB_], func=AF.Relu,
                    bias=fconst[:, 4:5], scale=fconst[:, 5:6]),
                    reads=(bkey,), writes=(f"m_rel{rp_}",))
                P.op("vector", lambda v, hp=hp, bank=bank, rp_=rp_, qt=qt: v.tensor_mul(
                    m_kkh[qt % 2].ap().rearrange("p a b -> p (a b)")[:, hp * 2 * SB_:
                                                                    (hp * 2 + 2) * SB_],
                    bank[:, 0:2 * SB_], m_rel[rp_][:, 0:2 * SB_]),
                    reads=(bkey, f"m_rel{rp_}"), writes=(f"m_kkh{qt % 2}",))
            for ch in range(NOH):
                bank, bkey = pbank[4 + ch], f"pb{4 + ch}"
                mm_chain(bank[:, 0:OHW], bkey,
                         [(m_kkh[qt % 2][:, hj, 0:128], wcvT[:, qt * (NHI // 4) + hj,
                                                            ch * OHW:(ch + 1) * OHW])
                          for hj in range(NHI // 4)],
                         (f"m_kkh{qt % 2}", "wcvT"), start=(qt == 0), stop=(qt == 3))
        for ch in range(NOH):
            bank, bkey = pbank[4 + ch], f"pb{4 + ch}"
            P.op("vector", lp(lambda v, ch=ch, bank=bank: v.tensor_mul(
                m_rkv[:, 0, ch * OHW:(ch + 1) * OHW], bank[:, 0:OHW],
                m_srk[:, 0, ch * OHW:(ch + 1) * OHW])),
                reads=(bkey, "m_srk"), writes=("m_rkv",))
        P.op("vector", lambda v, par=par: v.tensor_add(
            m_x2[par][:, 0, :], m_x2[par][:, 0, :], m_rkv[:, 0, :]),
            reads=(f"m_x2{par}", "m_rkv"), writes=(f"m_x2{par}",))
        P.dma("sync", "qsync",
              s_hbm[seg0:seg0 + SB_, :].rearrange("(a p) c -> p a c", p=128),
              m_x2[par][:], reads=(f"m_x2{par}",))

    a2_load(0)
    a2_stage1(0)
    a2_stage2(0)
    for s in range(1, g.NSB):
        a2_stage1(s)
        a2_stage3(s - 1)
        a2_stage2(s)
    a2_stage3(g.NSB - 1)
    P.barrier()

    P.dma("gpsimd", "qgps", f_wr[:], wrouter_in.rearrange("(a p) e -> p a e", p=128),
          writes=("f_wr",))
    P.dma("gpsimd", "qgps", f_wgt[:], wgating_in.rearrange("(a p) e -> p a e", p=128),
          writes=("f_wgt",))

    def router_ts2(ts):
        par = 0
        t0 = ts * TS
        P.dma("sync", "qsync", f_xnT[par][:],
              xnT_hbm.rearrange("(a p) t -> p a t", p=128)[:, :, t0:t0 + TS],
              writes=(f"f_xnT{par}",))
        for tt in range(NTT):
            gtt = ts * NTT + tt
            bank = pbank[6]
            mm_chain(bank[:, 0:E], "pb6",
                     [(f_xnT[par][:, ci, tt * 128:(tt + 1) * 128], f_wr[:, ci, :])
                      for ci in range(NCI)], (f"f_xnT{par}", "f_wr"))
            mm_chain(bank[:, 8:8 + E], "pb6b",
                     [(f_xnT[par][:, ci, tt * 128:(tt + 1) * 128], f_wgt[:, ci, :])
                      for ci in range(NCI)], (f"f_xnT{par}", "f_wgt"))
            rp = f_hact[:, 0, 0:E]
            gw = f_hact[:, 0, E:2 * E]
            for dst, dk, off, scale in [(rp, "f_rp", 0, None), (gw, "f_gw", 8, "ten")]:
                P.op("vector", lambda v, off=off: v.reduce_max(
                    f_smx[:, 0:1], bank[:, off:off + E], axis=mybir.AxisListType.X),
                    reads=("pb6", "pb6b"), writes=("f_smx",))
                P.op("vector", lambda v, dst=dst, off=off: v.tensor_single_scalar(
                    dst, bank[:, off:off + E], f_smx[:, 0:1], OP.subtract),
                    reads=("pb6", "pb6b", "f_smx"), writes=(dk,))
                if scale is None:
                    P.op("scalar", lambda s_, dst=dst: s_.activation(
                        out=dst, in_=dst, func=AF.Exp, bias=fconst[:, 4:5], scale=fconst[:, 5:6]), reads=(dk,), writes=(dk,))
                else:
                    P.op("scalar", lambda s_, dst=dst: s_.activation(
                        out=dst, in_=dst, func=AF.Exp, scale=fconst[:, 3:4], bias=fconst[:, 4:5]),
                        reads=(dk, "fconst"), writes=(dk,))
                P.op("vector", lambda v, dst=dst: v.reduce_sum(
                    f_smx[:, 1:2], dst, axis=mybir.AxisListType.X),
                    reads=(dk,), writes=("f_smx2",))
                P.op("vector", lambda v: v.reciprocal(f_smx[:, 2:3], f_smx[:, 1:2]),
                     reads=("f_smx2",), writes=("f_smx3",))
                P.op("vector", lambda v, dst=dst: v.tensor_single_scalar(
                    dst, dst, f_smx[:, 2:3], OP.mult), reads=(dk, "f_smx3"), writes=(dk,))
            P.op("vector", lambda v, gtt=gtt: v.tensor_mul(f_cw[:, gtt, :], rp, gw),
                 reads=("f_rp", "f_gw"), writes=("f_cw",))
            P.op("vector", lambda v, gtt=gtt: v.reduce_sum(
                f_smx[:, 1:2], f_cw[:, gtt, :], axis=mybir.AxisListType.X),
                reads=("f_cw",), writes=("f_smx2",))
            P.op("vector", lambda v: v.tensor_scalar_add(f_smx[:, 1:2], f_smx[:, 1:2], 1e-9),
                 reads=("f_smx2",), writes=("f_smx2",))
            P.op("vector", lambda v: v.reciprocal(f_smx[:, 2:3], f_smx[:, 1:2]),
                 reads=("f_smx2",), writes=("f_smx3",))
            P.op("vector", lambda v, gtt=gtt: v.tensor_single_scalar(
                f_cw[:, gtt, :], f_cw[:, gtt, :], f_smx[:, 2:3], OP.mult),
                reads=("f_cw", "f_smx3"), writes=("f_cw",))

    for ts in range(g.NTS):
        router_ts2(ts)

    wd_ctr = [0]
    def expert_ts(ts):
        par = 0
        t0 = ts * TS
        P.dma("sync", "qsync", f_xnT[par][:],
              xnT_hbm.rearrange("(a p) t -> p a t", p=128)[:, :, t0:t0 + TS],
              writes=(f"f_xnT{par}",))
        for e in range(E):
            for fc in range(g.F // FCH):
                wpar = fc % 2
                P.dma("gpsimd", "qgps", f_wg[wpar][:],
                      wgate_in[e, :, fc * FCH:(fc + 1) * FCH]
                      .rearrange("(a p) f -> p a f", p=128), writes=(f"f_wg{wpar}",))
                P.dma("gpsimd", "qgps", f_wu[wpar][:],
                      wup_in[e, :, fc * FCH:(fc + 1) * FCH]
                      .rearrange("(a p) f -> p a f", p=128), writes=(f"f_wu{wpar}",))
                THW = min(512, TS)
                for fj in range(FCH // 128):
                    fi = fc * (FCH // 128) + fj
                    for th in range(TS // THW):
                        ii = fi * (TS // THW) + th
                        bgi, bui = (0, 1, 2)[ii % 3], (3, 6, 7)[ii % 3]
                        bg, bgkey = pbank[bgi], f"pb{bgi}"
                        bu, bukey = pbank[bui], f"pb{bui}"
                        def gu_fn(t, fj=fj, th=th, wpar=wpar, par=par, bg=bg, bu=bu):
                            for ci in range(NCI):
                                t.matmul(bg[:, 0:THW],
                                         f_wg[wpar][:, ci, fj * 128:(fj + 1) * 128],
                                         f_xnT[par][:, ci, th * THW:(th + 1) * THW],
                                         start=(ci == 0), stop=(ci == NCI - 1))
                            ins = None
                            for ci in range(NCI):
                                ins = t.matmul(bu[:, 0:THW],
                                               f_wu[wpar][:, ci, fj * 128:(fj + 1) * 128],
                                               f_xnT[par][:, ci, th * THW:(th + 1) * THW],
                                               start=(ci == 0), stop=(ci == NCI - 1))
                            return ins
                        P.op("tensor", gu_fn,
                             reads=(f"f_wg{wpar}", f"f_wu{wpar}", f"f_xnT{par}"),
                             writes=(bgkey, bukey))
                        sp = ii % 2
                        P.op("scalar", lambda s_, bg=bg, sp=sp: s_.activation(
                            out=f_sil[sp][:, 0:THW], in_=bg[:, 0:THW], func=AF.Silu,
                            bias=fconst[:, 4:5], scale=fconst[:, 5:6]),
                            reads=(bgkey,), writes=(f"f_sil{sp}",))
                        P.op("vector", lambda v, fi=fi, th=th, bu=bu, sp=sp: v.tensor_mul(
                            f_hact[:, fi, th * THW:(th + 1) * THW], f_sil[sp][:, 0:THW],
                            bu[:, 0:THW]),
                            reads=(f"f_sil{sp}", bukey), writes=("f_hact",))
            NQ = NFI // 4
            chunks = [(hh, q) for hh in range(NOH) for q in range(4 // NOH * 2)][: 8]
            chunks = [(hh, q) for hh in range(NOH) for q in range(4)] if NOH == 2 else \
                     [(0, q) for q in range(4)]
            wd_slots = {}
            def emit_wd(idx):
                hh, q = chunks[idx]
                wi = wd_ctr[0] % 6
                wd_ctr[0] += 1
                wd_slots[(hh, q)] = wi
                P.dma("gpsimd", "qgps", f_wd[wi][:],
                      wdown_in[e, q * (g.F // 4):(q + 1) * (g.F // 4),
                               hh * OHW:(hh + 1) * OHW]
                      .rearrange("(a p) h -> p a h", p=128), writes=(f"f_wd{wi}",))
            nch = len(chunks)
            for idx in range(min(6, nch)):
                emit_wd(idx)
            for hh in range(NOH):
                for tt in range(NTT):
                    bdi = 4 + tt % 2
                    bank, bkey = pbank[bdi], f"pb{bdi}"
                    ws = [wd_slots[(hh, q)] for q in range(4)]
                    pa, pb_ = [], []
                    for q in range(4):
                        lst = pa if q < 2 else pb_
                        lst += [(f_hact[:, q * NQ + j, tt * 128:(tt + 1) * 128],
                                 f_wd[ws[q]][:, j, :]) for j in range(NQ)]
                    mm_chain(bank[:, 0:OHW], bkey, pa,
                             ("f_hact", f"f_wd{ws[0]}", f"f_wd{ws[1]}"),
                             start=True, stop=False)
                    if hh == 0 and tt == NTT - 1 and nch > 6:
                        for idx in range(6, nch):
                            emit_wd(idx)
                    mm_chain(bank[:, 0:OHW], bkey, pb_,
                             ("f_hact", f"f_wd{ws[2]}", f"f_wd{ws[3]}"),
                             start=False, stop=True)
                    gtt = ts * NTT + tt
                    if e == 0:
                        P.op("vector", lambda v, tt=tt, hh=hh, bank=bank, gtt=gtt, e=e:
                             v.tensor_single_scalar(
                                 f_acc[:, tt, hh * OHW:(hh + 1) * OHW], bank[:, 0:OHW],
                                 f_cw[:, gtt, e:e + 1], OP.mult),
                             reads=(bkey, "f_cw"), writes=("f_acc",))
                    else:
                        P.op("vector", lambda v, tt=tt, hh=hh, bank=bank, gtt=gtt, e=e:
                             v.scalar_tensor_tensor(
                                 f_acc[:, tt, hh * OHW:(hh + 1) * OHW], bank[:, 0:OHW],
                                 f_cw[:, gtt, e:e + 1],
                                 f_acc[:, tt, hh * OHW:(hh + 1) * OHW], OP.mult, OP.add),
                             reads=(bkey, "f_cw", "f_acc"), writes=("f_acc",))
        for tt in range(NTT):
            par2 = tt % 2
            r0 = t0 + tt * 128
            P.dma("sync", "qsync", f_x[par2][:], x_own[r0:r0 + 128, :],
                  writes=(f"f_x{par2}",))
            P.dma("sync", "qsync", f_s[par2][:], s_hbm[r0:r0 + 128, :],
                  writes=(f"f_s{par2}",))
            P.op("scalar", lambda s_, tt=tt, par2=par2: s_.activation(
                out=f_g[par2][:, :], in_=f_acc[:, tt, :], func=AF.Silu,
                bias=fconst[:, 4:5], scale=fconst[:, 5:6]),
                reads=("f_acc",), writes=(f"f_g{par2}",))
            P.op("vector", lambda v, par2=par2: v.tensor_mul(
                f_g[par2][:, :], f_g[par2][:, :], f_s[par2][:, :]),
                reads=(f"f_g{par2}", f"f_s{par2}"), writes=(f"f_g{par2}",))
            P.op("vector", lambda v, par2=par2: v.tensor_add(
                f_o[par2][:, :], f_g[par2][:, :], f_x[par2][:, :]),
                reads=(f"f_g{par2}", f"f_x{par2}"), writes=("f_o0",))
            P.dma("sync", "qsync", out_hbm[r0:r0 + 128, :], f_o[par2][:],
                  reads=("f_o0",))

    for ts in range(g.NTS):
        expert_ts(ts)

    P.finish()
    P.emit()
    return nc




from concourse.bass_utils import run_bass_kernel_spmd

_GEO = Geo()


def kernel(**inputs):
    g = _GEO
    T2 = g.T2
    shared = dict(
        mask0=np.zeros((128, 1), np.float32),
        mask1=np.ones((128, 1), np.float32),
        iota=np.tile(np.arange(max(g.SEG1, 128), dtype=np.float32), (128, 1)),
        identf=np.eye(128, dtype=np.float32),
        onescol=np.ones((128, 1), np.float32),
        onesrow=np.ones((1, 128), np.float32),
        fconst=np.tile(np.array([1e-5, -0.5, -1.0, 10.0, 0.0, 1.0], np.float32),
                       (128, 1)),
    )

    def v128(a):
        return np.ascontiguousarray(
            np.asarray(a).reshape(g.NCI, 128).T.astype(np.float32))

    wshared = dict(
        v_ln1g=v128(inputs["L0_attn_ln1_g"]), v_ln1b=v128(inputs["L0_attn_ln1_b"]),
        v_ln2g=v128(inputs["L0_attn_ln2_g"]), v_ln2b=v128(inputs["L0_attn_ln2_b"]),
        v_td=v128(inputs["L0_attn_time_decay"]),
        v_tf=v128(inputs["L0_attn_time_first"]),
        v_tmk=v128(inputs["L0_attn_tm_k"]), v_tmv=v128(inputs["L0_attn_tm_v"]),
        v_tmr=v128(inputs["L0_attn_tm_r"]),
        v_cmk=v128(inputs["L0_attn_cm_k"]), v_cmr=v128(inputs["L0_attn_cm_r"]),
        wkT=np.ascontiguousarray(inputs["L0_attn_Wk"].T),
        wvT=np.ascontiguousarray(inputs["L0_attn_Wv"].T),
        wrT=np.ascontiguousarray(inputs["L0_attn_Wr"].T),
        woT=np.ascontiguousarray(inputs["L0_attn_Wo"].T),
        wckT=np.ascontiguousarray(inputs["L0_attn_Wck"].T),
        wcvT=np.ascontiguousarray(inputs["L0_attn_Wcv"].T),
        wcrT=np.ascontiguousarray(inputs["L0_attn_Wcr"].T),
        wrouter=np.ascontiguousarray(inputs["L0_ffn_w_router"]),
        wgating=np.ascontiguousarray(inputs["L0_ffn_w_gating"]),
        wgate=np.ascontiguousarray(inputs["L0_ffn_w_gate"]),
        wup=np.ascontiguousarray(inputs["L0_ffn_w_up"]),
        wdown=np.ascontiguousarray(inputs["L0_ffn_w_down"]),
    )

    x = np.asarray(inputs["L0_attn_x"], np.float32)
    B = x.shape[0]
    prefs = [np.ascontiguousarray(x[b, 0:T2, :]) for b in range(B)]

    nc = build_nc(g)
    in_maps = []
    for b in range(B):
        for h in range(2):
            m = dict(shared)
            m["mask"] = shared["mask1"] if h else shared["mask0"]
            m.pop("mask0"), m.pop("mask1")
            m.update(wshared)
            m["x_own"] = np.ascontiguousarray(x[b, h * T2:(h + 1) * T2, :])
            m["x_pref"] = prefs[b]
            in_maps.append(m)

    res = run_bass_kernel_spmd(nc, in_maps, core_ids=list(range(8)))
    out = np.empty_like(x)
    for b in range(B):
        for h in range(2):
            out[b, h * T2:(h + 1) * T2, :] = np.asarray(res.results[b * 2 + h]["out"])
    return out
